# version 13
# baseline (speedup 1.0000x reference)
import contextlib
import numpy as np

import concourse.bass as bass
import concourse.mybir as mybir
from concourse.bass_utils import run_bass_kernel_spmd

F32 = mybir.dt.float32
BF16 = mybir.dt.bfloat16
I32 = mybir.dt.int32
AF = mybir.ActivationFunctionType
ALU = mybir.AluOpType
AX = mybir.AxisListType

P = 128
DM = 1024
NDP = DM // P
HD = 64
NHC = 4
QK = NHC * HD
F1 = 4096
NFP = F1 // P
EPS = 1e-5
SCALE = 0.125

ENGS = ("tensor", "vector", "scalar", "sync", "gpsimd")
SEM_OF = dict(tensor="T", vector="V", scalar="S")
DMA_SEMS = ("Dpro", "Dwp", "Dx", "Dx2", "Dsc", "Drin", "Dq", "Dqo",
            "Dw1a", "Dw1b", "Dw2a", "Dw2b", "Dxa", "Dxb", "Drs", "Dout")
ALL_SEMS = ("T", "V", "S", "Gp", "CC") + DMA_SEMS


class Rec:
    def __init__(self):
        self.q = {e: [] for e in ENGS}
        self.n = {s: 0 for s in ALL_SEMS}

    def op(self, eng, fn, waits=(), sem=None):
        s = sem or SEM_OF[eng]
        a = 16 if s.startswith("D") else 1
        if eng == "gpsimd" and s == "Gp":
            pass
        waits = list(waits)
        if eng == "vector":
            waits.append(("V", self.n["V"]))
        elif eng == "scalar":
            waits.append(("S", self.n["S"]))
        self.n[s] += a
        w = tuple((ws, wv) for ws, wv in waits if wv > 0)
        self.q[eng].append((fn, w, s, a))
        return self.n[s]


def t_of(h):
    return h[:, :].tensor


def build(T):
    NG = T // 1024
    NKT = T // 128
    TQ = T // 4
    NST = TQ // 512
    assert NST >= 1 and T % 1024 == 0

    nc = bass.Bass("TRN2", target_bir_lowering=False, debug=False)
    ap_xb = nc.dram_tensor("xb", [T, DM], F32, kind="ExternalInput").ap()
    ap_xq = nc.dram_tensor("xq", [TQ, DM], F32, kind="ExternalInput").ap()
    ap_wq = nc.dram_tensor("wq", [DM, QK], F32, kind="ExternalInput").ap()
    ap_wk = nc.dram_tensor("wk", [DM, QK], F32, kind="ExternalInput").ap()
    ap_wv = nc.dram_tensor("wv", [DM, QK], F32, kind="ExternalInput").ap()
    ap_bq = nc.dram_tensor("bq", [P, 2], F32, kind="ExternalInput").ap()
    ap_bk = nc.dram_tensor("bk", [P, 2], F32, kind="ExternalInput").ap()
    ap_bv = nc.dram_tensor("bv", [P, 2], F32, kind="ExternalInput").ap()
    ap_wp = nc.dram_tensor("wp", [QK, DM], F32, kind="ExternalInput").ap()
    ap_bpb = nc.dram_tensor("bpb", [1, DM], F32, kind="ExternalInput").ap()
    ap_gam = nc.dram_tensor("gam", [P, NDP], F32, kind="ExternalInput").ap()
    ap_bet = nc.dram_tensor("bet", [P, NDP], F32, kind="ExternalInput").ap()
    ap_q1 = nc.dram_tensor("q1", [DM, F1], I32, kind="ExternalInput").ap()
    ap_s1 = nc.dram_tensor("s1", [NDP, F1], F32, kind="ExternalInput").ap()
    ap_b1t = nc.dram_tensor("b1t", [P, NFP], F32, kind="ExternalInput").ap()
    ap_q2 = nc.dram_tensor("q2", [F1, DM], I32, kind="ExternalInput").ap()
    ap_s2 = nc.dram_tensor("s2", [NFP, DM], F32, kind="ExternalInput").ap()
    ap_b2t = nc.dram_tensor("b2t", [P, NDP], F32, kind="ExternalInput").ap()
    ap_idf = nc.dram_tensor("idf", [P, P], F32, kind="ExternalInput").ap()
    ap_onesr = nc.dram_tensor("onesr", [1, P], F32, kind="ExternalInput").ap()
    ap_onesc = nc.dram_tensor("onesc", [P, 1], F32, kind="ExternalInput").ap()
    ap_trim = nc.dram_tensor("trim", [P, P], F32, kind="ExternalInput").ap()
    ap_cv = nc.dram_tensor("cvals", [1, 8], F32, kind="ExternalInput").ap()
    ap_out = nc.dram_tensor("out_q", [TQ, DM], F32, kind="ExternalOutput").ap()

    rin = nc.dram_tensor("rin", [T, DM], F32)
    rsout = nc.dram_tensor("rsout", [TQ, DM], F32)
    w1s = nc.dram_tensor("w1s", [DM, F1], BF16)
    w2s = nc.dram_tensor("w2s", [F1, DM], BF16)

    r = Rec()

    with contextlib.ExitStack() as perm:
        pe = perm.enter_context
        sems = {s: pe(nc.semaphore("sem" + s)) for s in ALL_SEMS}

        def emit_block():
            blk = nc.Block()
            with blk as block:
                def mk(eng_name):
                    def _replay(e):
                        for fn, waits, s, a in r.q[eng_name]:
                            for ws, wv in waits:
                                e.wait_ge(sems[ws], wv)
                            fn(e).then_inc(sems[s], a)
                    return _replay
                block.tensor(mk("tensor"))
                block.vector(mk("vector"))
                block.scalar(mk("scalar"))
                block.sync(mk("sync"))
                block.gpsimd(mk("gpsimd"))
            for e in ENGS:
                r.q[e] = []

        idf_sb = pe(nc.sbuf_tensor("idf_sb", [P, P], F32))
        idb = pe(nc.sbuf_tensor("idb", [P, P], BF16))
        onesr_b = pe(nc.sbuf_tensor("onesr_b", [1, P], BF16))
        onesc_b = pe(nc.sbuf_tensor("onesc_b", [P, 1], BF16))
        trib = pe(nc.sbuf_tensor("trib", [P, P], BF16))
        stag = pe(nc.sbuf_tensor("stag", [P, 2 * P + 2], F32))
        bq_sb = pe(nc.sbuf_tensor("bq_sb", [P, 2], F32))
        bk_sb = pe(nc.sbuf_tensor("bk_sb", [P, 2], F32))
        bv_sb = pe(nc.sbuf_tensor("bv_sb", [P, 2], F32))
        gam_sb = pe(nc.sbuf_tensor("gam_sb", [P, NDP], F32))
        bet_sb = pe(nc.sbuf_tensor("bet_sb", [P, NDP], F32))
        betb = pe(nc.sbuf_tensor("betb", [P, NDP], BF16))
        b1t_sb = pe(nc.sbuf_tensor("b1t_sb", [P, NFP], F32))
        bias1 = pe(nc.sbuf_tensor("bias1", [P, NFP], F32))
        b2t_sb = pe(nc.sbuf_tensor("b2t_sb", [P, NDP], F32))
        biasM = pe(nc.sbuf_tensor("biasM", [P, 8], F32))
        statr = pe(nc.sbuf_tensor("statr", [1, 16], F32))
        ctile = pe(nc.sbuf_tensor("ctile", [P, 8], F32))
        for i, val in enumerate((EPS, -0.5, SCALE, 1.0, 0.0)):
            nc.const_aps.register(F32, val, ctile[:, i:i + 1])

        with contextlib.ExitStack() as sw:
            se = sw.enter_context
            QT = se(nc.sbuf_tensor("QT", [P, 2 * T], BF16))
            KT = se(nc.sbuf_tensor("KT", [P, 2 * T], BF16))
            VS = se(nc.sbuf_tensor("VS", [P, NKT * 260], BF16))
            ost = se(nc.sbuf_tensor("ost", [P, 1024], BF16))
            xs = [se(nc.sbuf_tensor(f"xs{i}", [P, 4 * DM], F32)) for i in range(2)]
            xnb = [se(nc.sbuf_tensor(f"xnb{i}", [P, 2 * DM], F32)) for i in range(2)]
            xnT = se(nc.sbuf_tensor("xnT", [P, NDP * 512], BF16))
            vts = se(nc.sbuf_tensor("vts", [P, 1024], F32))
            PT = [se(nc.sbuf_tensor(f"PT{i}", [P, 1024], BF16)) for i in range(2)]
            sqb = se(nc.sbuf_tensor("sqb", [P, 512], BF16))
            wsb = se(nc.sbuf_tensor("wsb", [P, NDP * 768], BF16))
            wpsb = se(nc.sbuf_tensor("wpsb", [P, 2 * DM], BF16))
            prsb = se(nc.sbuf_tensor("prsb", [P, 2 * DM], F32))
            lnst = se(nc.sbuf_tensor("lnst", [P, 80], F32))
            rrb = se(nc.sbuf_tensor("rrb", [1, 1024], BF16))
            rbsb = se(nc.sbuf_tensor("rbsb", [P, 512], F32))
            q1f = se(nc.sbuf_tensor("q1f", [P, 1024], F32))
            s1bf = se(nc.sbuf_tensor("s1bf", [P, 1024], F32))
            w1bf = se(nc.sbuf_tensor("w1bf", [P, 1024], BF16))
            mrow = se(nc.sbuf_tensor("mrow", [1, 4], BF16))

            pS = se(nc.psum_tensor("pS", [P, 2048], F32))
            pPV = [se(nc.psum_tensor(f"pPV{i}", [P, 512], F32)) for i in range(2)]
            pT = se(nc.psum_tensor("pT", [P, 1024], F32))

            r.op("sync", lambda e: e.dma_start(
                out=ctile[:, :], in_=bass.AP(ap_cv.tensor, 0, [[0, P], [1, 8]])),
                sem="Dpro")
            for src, dst in ((ap_bq, bq_sb), (ap_bk, bk_sb), (ap_bv, bv_sb),
                             (ap_gam, gam_sb), (ap_bet, bet_sb),
                             (ap_b1t, b1t_sb), (ap_b2t, b2t_sb)):
                r.op("sync", lambda e, s=src, d=dst: e.dma_start(out=d[:, :], in_=s),
                     sem="Dpro")
            r.op("sync", lambda e: e.dma_start(out=idf_sb[:, :], in_=ap_idf), sem="Dpro")
            r.op("sync", lambda e: e.dma_start(out=stag[:, 0:P], in_=ap_trim), sem="Dpro")
            r.op("sync", lambda e: e.dma_start(out=stag[0:1, P:2 * P], in_=ap_onesr),
                 sem="Dpro")
            pro = r.op("sync", lambda e: e.dma_start(
                out=stag[:, 2 * P:2 * P + 1], in_=ap_onesc), sem="Dpro")
            r.op("scalar", lambda e: e.nop(), waits=[("Dpro", pro)])
            c_id = r.op("vector", lambda e: e.tensor_copy(out=idb[:, :], in_=idf_sb[:, :]),
                        waits=[("Dpro", pro)])
            c_or = r.op("vector", lambda e: e.tensor_copy(out=onesr_b[:, :],
                                                          in_=stag[0:1, P:2 * P]))
            c_oc = r.op("vector", lambda e: e.tensor_copy(out=onesc_b[:, :],
                                                          in_=stag[:, 2 * P:2 * P + 1]))
            c_tri = r.op("vector", lambda e: e.tensor_copy(out=trib[:, :],
                                                           in_=stag[:, 0:P]))
            c_bet = r.op("vector", lambda e: e.tensor_copy(out=betb[:, :],
                                                           in_=bet_sb[:, :]))
            for src, cb0 in ((ap_wq, 0), (ap_wk, 2), (ap_wv, 4)):
                dst = wsb[:, :].rearrange("p (dp c) -> p dp c", dp=NDP)[
                    :, :, cb0 * P: cb0 * P + 256]
                r.op("gpsimd", lambda e, s=src, d=dst: e.dma_start(
                    out=d, in_=s.rearrange("(dp p) c -> p dp c", p=P)), sem="Dwp")
            gw = r.op("gpsimd", lambda e: e.dma_start(
                out=wpsb[:, :].rearrange("p (a c) -> p a c", a=2),
                in_=ap_wp.rearrange("(a p) c -> p a c", p=P)), sem="Dwp")
            vm = 0
            for h in range(NHC):
                vm = r.op("vector", lambda e, h=h: e.memset(
                    VS[:, :].rearrange("p (kt s) -> p kt s", kt=NKT)[
                        :, :, 65 * h + 64: 65 * h + 65], 1.0))
            z0 = r.op("vector", lambda e: e.memset(statr[0:1, :], 0.0))
            z1 = r.op("vector", lambda e: e.tensor_copy(
                out=bias1[:, :], in_=b1t_sb[:, :]), waits=[("Dpro", 16 * 12)])

            last = dict(
                xs_v=[0, 0], xnT_t=0, xnb_rd=[0, 0], vs_copy=0, pt_v=[0] * 16, pt_seq=0,
                pt_prev=[0, 0], pS_exp_sl=[0, 0],
                pS_free=[0] * 4, pS_exp=0, pW_free=0, pPV_free=[0, 0],
                PT_t=[0, 0], sq_t=0, ost_t=0, prsb_d=0,
                dq_v=0, dq_go=0, dq_pend=None, dq_ld=(0, 0), b1_v=z1, stat_v=z0,
                biasM_v=0, biasM_s=0,
            )

            def ln_stats(waits0, xsap, c0):
                w0 = r.op("vector", lambda e, c0=c0: e.bn_stats(
                    out=lnst[:, c0:c0 + 6], in_=xsap[:, 0:512]), waits=waits0)
                w1 = r.op("vector", lambda e, c0=c0: e.bn_stats(
                    out=lnst[:, c0 + 6:c0 + 12], in_=xsap[:, 512:1024]),
                    waits=[("V", w0)])
                return r.op("vector", lambda e, c0=c0: e.bn_aggr(
                    out=lnst[:, c0 + 12:c0 + 14], in_=lnst[:, c0:c0 + 12]),
                    waits=[("V", w1)])

            def ln_fin(wstat, xsap, xnap, c0):
                s0 = r.op("scalar", lambda e, c0=c0: e.activation(
                    out=lnst[:, c0 + 14:c0 + 15], in_=lnst[:, c0 + 13:c0 + 14],
                    func=AF.Ln, bias=EPS, scale=1.0), waits=[("V", wstat)])
                s1_ = r.op("scalar", lambda e, c0=c0: e.activation(
                    out=lnst[:, c0 + 15:c0 + 16], in_=lnst[:, c0 + 14:c0 + 15],
                    func=AF.Exp, bias=0.0, scale=-0.5), waits=[("S", s0)])
                return r.op("vector", lambda e, c0=c0: e.tensor_scalar(
                    out=xnap, in0=xsap, scalar1=lnst[:, c0 + 12:c0 + 13],
                    scalar2=lnst[:, c0 + 15:c0 + 16], op0=ALU.subtract,
                    op1=ALU.mult), waits=[("S", s1_)])

            def ln_tile(waits0, xsap, xnap):
                w2 = ln_stats(waits0, xsap, 64)
                return ln_fin(w2, xsap, xnap, 64)

            ptwr = [0, 0]
            ptrd = [("V", 0), ("V", 0)]
            last["lnv"] = [0, 0, 0, 0]
            last["lnw"] = [0, 0, 0, 0]
            last["dx"] = ("Dx", 0)
            last["xnT_s"] = 0
            last["qkev3"] = 0
            last["biasM_vp"] = [0, 0, 0, 0]
            last["biasM_sp"] = [0, 0, 0, 0]
            prod_done = {}

            def pt_bank():
                b = last["pt_seq"] % 2
                last["pt_seq"] += 1
                return b

            def make_prod_units(G):
                units = []

                def U(f):
                    units.append(f)
                for sg in range(2):
                    base = 1024 * G + 512 * sg
                    slot = (2 * G + sg) % 2

                    def u_dma(base=base, slot=slot):
                        dxsem = "Dx" if slot == 0 else "Dx2"
                        dx = r.op("sync", lambda e, b=base, s=slot: e.dma_start(
                            out=xs[s][:, :].rearrange("p (a c) -> p a c", a=4),
                            in_=ap_xb[b:b + 512, :].rearrange(
                                "(a p) c -> p a c", p=P)),
                            waits=[("V", last["xs_v"][slot])], sem=dxsem)
                        last["dx"] = (dxsem, dx)
                    U(u_dma)
                    for tt in range(4):
                        def u_lns(tt=tt, slot=slot):
                            last["lnw"][tt] = ln_stats(
                                [last["dx"]],
                                xs[slot][:, tt * DM:(tt + 1) * DM], 16 * tt)
                        U(u_lns)
                    for tt in range(4):
                        def u_lnf(tt=tt, slot=slot):
                            hs = tt // 2
                            if tt % 2 == 0:
                                r.op("vector", lambda e: e.engine_nop(),
                                     waits=[("T", last["xnb_rd"][hs])])
                            last["lnv"][tt] = ln_fin(
                                last["lnw"][tt],
                                xs[slot][:, tt * DM:(tt + 1) * DM],
                                xnb[hs][:, (tt % 2) * DM:(tt % 2 + 1) * DM], 16 * tt)
                            if tt == 3:
                                last["xs_v"][slot] = last["lnv"][3]
                        U(u_lnf)
                    for tt in range(4):
                        for dpg in range(2):
                            def u_tr(tt=tt, dpg=dpg):
                                bnk = pt_bank()
                                t1 = 0
                                for dpl in range(4):
                                    dp = dpg * 4 + dpl
                                    tw = [("V", last["lnv"][tt]), ("Dpro", pro)]
                                    if dpl == 0:
                                        tw.append(ptrd[bnk])
                                    t1 = r.op("tensor", lambda e, tt=tt, dp=dp,
                                              bnk=bnk, dpl=dpl: e.transpose(
                                        pT[:, bnk * 512 + dpl * P: bnk * 512 + (dpl + 1) * P],
                                        xnb[tt // 2][:, (tt % 2) * DM + dp * P:
                                                      (tt % 2) * DM + (dp + 1) * P],
                                        idf_sb[:, :]), waits=tw)
                                cp = r.op("scalar", lambda e, tt=tt, dpg=dpg,
                                          bnk=bnk: e.copy(
                                    out=bass.AP(t_of(xnT), (dpg * 4) * 512 + tt * P,
                                                [[512 * NDP, P], [512, 4], [1, P]]),
                                    in_=pT[:, bnk * 512: bnk * 512 + 512]),
                                    waits=[("T", t1)])
                                ptwr[bnk] = t1
                                ptrd[bnk] = ("S", cp)
                                last["xnb_rd"][tt // 2] = t1
                                last["xnT_s"] = cp
                            U(u_tr)
                    for cb in range(6):
                        def u_mm(cb=cb, base=base):
                            bnk = pt_bank()
                            ps_reg = pT[:, bnk * 512: bnk * 512 + 512]
                            tm = 0
                            for dp in range(NDP):
                                tw = [("S", last["xnT_s"])]
                                if dp == 0:
                                    tw += [ptrd[bnk], ("Dwp", gw)]
                                tm = r.op("tensor", lambda e, cb=cb, dp=dp,
                                          ps=ps_reg: e.matmul(
                                    ps, wsb[:, dp * 768 + cb * P: dp * 768 + (cb + 1) * P],
                                    xnT[:, dp * 512:(dp + 1) * 512],
                                    start=(dp == 0), stop=(dp == NDP - 1)), waits=tw)
                            if cb < 2:
                                dst = QT[:, cb * T + base: cb * T + base + 512]
                                bap = bq_sb[:, cb:cb + 1]
                            elif cb < 4:
                                dst = KT[:, (cb - 2) * T + base: (cb - 2) * T + base + 512]
                                bap = bk_sb[:, cb - 2:cb - 1]
                            else:
                                dst = vts[:, (cb - 4) * 512:(cb - 4) * 512 + 512]
                                bap = bv_sb[:, cb - 4:cb - 3]
                            ev = r.op("vector", lambda e, d=dst, b=bap, ps=ps_reg:
                                      e.tensor_scalar(out=d, in0=ps, scalar1=b,
                                                      scalar2=None, op0=ALU.add),
                                      waits=[("T", tm), ("Dpro", pro)])
                            ptwr[bnk] = tm
                            ptrd[bnk] = ("V", ev)
                            last["xnT_t"] = tm
                            if cb == 3:
                                last["qkev3"] = ev
                            last["qkev_" + str(cb)] = ev
                        U(u_mm)
                    for vp in range(2):
                        def u_vt(vp=vp, sg=sg):
                            kt0 = G * 8 + sg * 4
                            bnk = pt_bank()
                            t1 = 0
                            for j in range(4):
                                tw = [("V", last["qkev_" + str(4 + vp)]),
                                      ("Dpro", pro)]
                                if j == 0:
                                    tw.append(ptrd[bnk])
                                t1 = r.op("tensor", lambda e, vp=vp, j=j, bnk=bnk:
                                          e.transpose(
                                    pT[:, bnk * 512 + j * P: bnk * 512 + (j + 1) * P],
                                    vts[:, vp * 512 + j * P: vp * 512 + (j + 1) * P],
                                    idf_sb[:, :]), waits=tw)
                            cpv = r.op("scalar", lambda e, kt0=kt0, vp=vp, bnk=bnk:
                                       e.copy(
                                out=bass.AP(t_of(VS), kt0 * 260 + 65 * (2 * vp),
                                            [[NKT * 260, P],
                                             [260, 4], [65, 2], [1, 64]]),
                                in_=bass.AP(t_of(pT), bnk * 512,
                                            [[1024, P], [P, 4], [64, 2], [1, 64]])),
                                waits=[("T", t1), ("V", vm)])
                            ptwr[bnk] = t1
                            ptrd[bnk] = ("S", cpv)
                            last["vs_copy"] = cpv
                        U(u_vt)
                    for side in range(2):
                        for cbp in range(2):
                            def u_st(side=side, cbp=cbp, base=base):
                                tens = QT if side == 0 else KT
                                col0 = 0 if side == 0 else 4
                                srcap = tens[:, cbp * T + base: cbp * T + base + 512]
                                sv = r.op("vector", lambda e, s=srcap: e.tensor_tensor(
                                    out=sqb[:, :], in0=s, in1=s, op=ALU.mult),
                                    waits=[("V", last["qkev_" + str(side * 2 + cbp)]),
                                           ("T", last["sq_t"])])
                                bnks = [pt_bank(), pt_bank()]
                                tmn = 0
                                for hh in range(2):
                                    tw = [("V", sv), ("V", c_oc), ptrd[bnks[hh]]]
                                    tmn = r.op("tensor", lambda e, hh=hh,
                                               b=bnks[hh]: e.matmul(
                                        pT[0:1, b * 512: b * 512 + 512],
                                        onesc_b[64 * hh:64 * hh + 64, 0:1],
                                        sqb[64 * hh:64 * hh + 64, :],
                                        start=True, stop=True,
                                        tile_position=(64 * hh, 0)), waits=tw)
                                    ptwr[bnks[hh]] = tmn
                                last["sq_t"] = tmn
                                rv = 0
                                for hh in range(2):
                                    h = 2 * cbp + hh
                                    rv = r.op("vector", lambda e, h=h,
                                              b=bnks[hh]: e.tensor_reduce(
                                        out=lnst[0:1, 8 + h:9 + h],
                                        in_=pT[0:1, b * 512: b * 512 + 512],
                                        axis=AX.X, op=ALU.max),
                                        waits=[("T", tmn), ("V", last["stat_v"])])
                                    mxs = r.op("vector", lambda e, h=h, c=col0:
                                               e.tensor_tensor(
                                        out=statr[0:1, c + h:c + h + 1],
                                        in0=statr[0:1, c + h:c + h + 1],
                                        in1=lnst[0:1, 8 + h:9 + h], op=ALU.max),
                                        waits=[("V", rv), ("V", z0)])
                                    last["stat_v"] = mxs
                                ptrd[bnks[0]] = ("V", last["stat_v"])
                                ptrd[bnks[1]] = ("V", last["stat_v"])
                            U(u_st)

                def u_mhat(G=G):
                    par = G % 4
                    m0 = r.op("vector", lambda e: e.tensor_tensor(
                        out=statr[0:1, 8:12], in0=statr[0:1, 0:4],
                        in1=statr[0:1, 4:8], op=ALU.add),
                        waits=[("V", last["stat_v"])])
                    m1 = r.op("vector", lambda e: e.tensor_reduce(
                        out=statr[0:1, 12:14].rearrange("p (a b) -> p a b", b=1),
                        in_=statr[0:1, 8:12].rearrange("p (a b) -> p a b", b=2),
                        axis=AX.X, op=ALU.max), waits=[("V", m0)])
                    m2 = r.op("vector", lambda e: e.tensor_scalar(
                        out=statr[0:1, 14:16], in0=statr[0:1, 12:14],
                        scalar1=-0.5 * SCALE * 1.01, scalar2=None, op0=ALU.mult),
                        waits=[("V", m1)])
                    m3 = r.op("vector", lambda e: e.tensor_copy(
                        out=mrow[0:1, 0:2], in_=statr[0:1, 14:16]), waits=[("V", m2)])
                    last["stat_v"] = m3
                    bnk = pt_bank()
                    bm = r.op("tensor", lambda e, bnk=bnk: e.matmul(
                        pT[:, bnk * 512: bnk * 512 + 2], onesr_b[0:1, :],
                        mrow[0:1, 0:2], start=True, stop=True),
                        waits=[("V", m3), ptrd[bnk], ("V", c_or),
                               ("S", last["biasM_sp"][par])])
                    bmv = r.op("vector", lambda e, par=par, bnk=bnk: e.tensor_copy(
                        out=biasM[:, par * 2: par * 2 + 2],
                        in_=pT[:, bnk * 512: bnk * 512 + 2]), waits=[("T", bm)])
                    ptwr[bnk] = bm
                    ptrd[bnk] = ("V", bmv)
                    last["biasM_vp"][par] = bmv
                    prod_done[G] = dict(vs=last["vs_copy"], qkev3=last["qkev3"],
                                        bmv=bmv)
                U(u_mhat)

                return units

            def make_dq_units(G):
                units = []
                nw1 = 4 * NDP
                tot = nw1 + NFP

                def u_load(ci):
                    if ci < nw1:
                        g, half = ci // 4, ci % 4
                        gq = r.op("gpsimd", lambda e, g=g, half=half: e.dma_start(
                            out=q1f[:, :], in_=ap_q1[g * P:(g + 1) * P,
                                                     half * 1024:(half + 1) * 1024]),
                            waits=[("V", last["dq_v"])], sem="Dq")
                        sbp = bass.AP(ap_s1.tensor, g * F1 + half * 1024,
                                      [[0, P], [1, 1024]])
                        gs = r.op("gpsimd", lambda e, sbp=sbp: e.dma_start(
                            out=s1bf[:, :], in_=sbp),
                            waits=[("V", last["dq_v"])], sem="Dsc")
                    else:
                        g2 = ci - nw1
                        gq = r.op("gpsimd", lambda e, g2=g2: e.dma_start(
                            out=q1f[:, 0:DM], in_=ap_q2[g2 * P:(g2 + 1) * P, :]),
                            waits=[("V", last["dq_v"])], sem="Dq")
                        sbp = bass.AP(ap_s2.tensor, g2 * DM, [[0, P], [1, DM]])
                        gs = r.op("gpsimd", lambda e, sbp=sbp: e.dma_start(
                            out=s1bf[:, 0:DM], in_=sbp),
                            waits=[("V", last["dq_v"])], sem="Dsc")
                    last["dq_ld"] = (gq, gs)

                def u_comp(ci):
                    gq, gs = last["dq_ld"]
                    if ci < nw1:
                        g, half = ci // 4, ci % 4
                        dq = r.op("vector", lambda e: e.scalar_tensor_tensor(
                            out=w1bf[:, :], in0=q1f[:, :], scalar=-128.0,
                            in1=s1bf[:, :], op0=ALU.add, op1=ALU.mult),
                            waits=[("Dq", gq), ("Dsc", gs), ("Dqo", last["dq_go"])])
                        bnk = pt_bank()
                        tbw = 0
                        for fl in range(8):
                            tw = [("V", dq), ("V", c_bet)]
                            if fl == 0:
                                tw.append(ptrd[bnk])
                            tbw = r.op("tensor", lambda e, g=g, fl=fl, bnk=bnk:
                                       e.matmul(
                                pT[:, bnk * 512 + fl: bnk * 512 + fl + 1],
                                w1bf[:, fl * P:(fl + 1) * P],
                                betb[:, g:g + 1],
                                start=True, stop=True), waits=tw)
                        ab = r.op("vector", lambda e, half=half, bnk=bnk:
                                  e.tensor_tensor(
                            out=bias1[:, half * 8:(half + 1) * 8],
                            in0=bias1[:, half * 8:(half + 1) * 8],
                            in1=pT[:, bnk * 512: bnk * 512 + 8], op=ALU.add),
                            waits=[("T", tbw), ("V", last["b1_v"])])
                        ptwr[bnk] = tbw
                        ptrd[bnk] = ("V", ab)
                        last["b1_v"] = ab
                        gm = r.op("vector", lambda e, g=g: e.tensor_scalar(
                            out=w1bf[:, :], in0=w1bf[:, :],
                            scalar1=gam_sb[:, g:g + 1],
                            scalar2=None, op0=ALU.mult),
                            waits=[("T", tbw), ("V", ab), ("Dpro", pro)])
                        go = r.op("gpsimd", lambda e, g=g, half=half: e.dma_start(
                            out=w1s.ap()[g * P:(g + 1) * P,
                                         half * 1024:(half + 1) * 1024],
                            in_=w1bf[:, :]), waits=[("V", gm)], sem="Dqo")
                        last["dq_v"] = gm
                        last["dq_go"] = go
                    else:
                        g2 = ci - nw1
                        dq = r.op("vector", lambda e: e.scalar_tensor_tensor(
                            out=w1bf[:, 0:DM], in0=q1f[:, 0:DM], scalar=-128.0,
                            in1=s1bf[:, 0:DM], op0=ALU.add, op1=ALU.mult),
                            waits=[("Dq", gq), ("Dsc", gs), ("Dqo", last["dq_go"])])
                        go = r.op("gpsimd", lambda e, g2=g2: e.dma_start(
                            out=w2s.ap()[g2 * P:(g2 + 1) * P, :],
                            in_=w1bf[:, 0:DM]),
                            waits=[("V", dq)], sem="Dqo")
                        last["dq_v"] = dq
                        last["dq_go"] = go

                for ci in range(tot * G // NG, tot * (G + 1) // NG):
                    def u_a(ci=ci):
                        if last["dq_pend"] is not None:
                            u_comp(last["dq_pend"])
                        u_load(ci)
                        last["dq_pend"] = ci
                    units.append(u_a)
                return units
            pending = []
            pending_dq = []
            fed = [0]

            def feed(upto):
                while fed[0] <= min(upto, NG - 1):
                    for u in make_prod_units(fed[0]):
                        pending.append((fed[0], u))
                    fed[0] += 1

            def flush(grp):
                while pending and pending[0][0] <= grp:
                    pending.pop(0)[1]()

            feed(0)
            flush(0)
            for G in range(NG):
                feed(G + 2)
                flush(G)
                pending_dq += make_dq_units(G)
                qk_ready = prod_done[G]["vs"]
                qk_ready_v = prod_done[G]["qkev3"]
                for qc in (2 * G, 2 * G + 1):
                    nm = [0, 0]
                    for hp in range(2):
                        h0, h1 = 2 * hp, 2 * hp + 1
                        nkt = 4 * qc + 4
                        pvstart = [True, True]
                        tpv = 0
                        pend = None
                        def emit_pv(e_kt, e_off, e_ptsl, e_exp, e_mw):
                            nonlocal tpv
                            for hh, h in ((0, h0), (1, h1)):
                                tw = [("S", e_exp)]
                                if e_mw:
                                    tw.append(("V", e_mw))
                                if pvstart[hh]:
                                    tw.append(("V", last["pPV_free"][hh]))
                                tpv = r.op("tensor", lambda e, kt=e_kt, h=h, hh=hh,
                                           off=e_off, ptsl=e_ptsl,
                                           st=pvstart[hh], sp=(e_kt == nkt - 1):
                                           e.matmul(
                                    pPV[hh][0:65, off:512],
                                    VS[:, kt * 260 + 65 * h: kt * 260 + 65 * h + 65],
                                    PT[ptsl][:, hh * 512 + off: hh * 512 + 512],
                                    start=st, stop=sp), waits=tw)
                                pvstart[hh] = False
                            last["PT_t"][e_ptsl] = tpv
                        for kt in range(nkt):
                            for _ in range(6):
                                if pending:
                                    pending.pop(0)[1]()
                            sl = kt % 2
                            base_c = sl * 1024
                            off = max(0, P * (kt - 4 * qc))
                            tmm = 0
                            for hh, h in ((0, h0), (1, h1)):
                                tw = []
                                if hh == 0:
                                    tw = [("S", last["pS_exp_sl"][sl]),
                                          ("V", last["pS_free"][2 * sl]),
                                          ("V", last["pS_free"][2 * sl + 1]),
                                          ("S", qk_ready), ("V", qk_ready_v)]
                                tmm = r.op("tensor", lambda e, h=h, kt=kt, hh=hh,
                                           base_c=base_c, qc=qc: e.matmul(
                                    pS[:, base_c + hh * 512: base_c + hh * 512 + 512],
                                    KT[64 * (h % 2):64 * (h % 2) + 64,
                                       (h // 2) * T + kt * P:(h // 2) * T + (kt + 1) * P],
                                    QT[64 * (h % 2):64 * (h % 2) + 64,
                                       (h // 2) * T + qc * 512:(h // 2) * T + (qc + 1) * 512],
                                    start=True, stop=True,
                                    tile_position=(64 * (h % 2), 0)), waits=tw)
                            ptsl = kt % 2
                            exp_s = r.op("scalar", lambda e, ptsl=ptsl, hp=hp,
                                         base_c=base_c, par=G % 4: e.activation(
                                out=PT[ptsl][:, :], in_=pS[:, base_c:base_c + 1024],
                                func=AF.Exp,
                                bias=biasM[:, par * 2 + hp: par * 2 + hp + 1],
                                scale=SCALE),
                                waits=[("T", tmm), ("V", prod_done[G]["bmv"]),
                                       ("T", last["PT_t"][ptsl])])
                            last["pS_exp_sl"][sl] = exp_s
                            last["pS_exp"] = exp_s
                            last["biasM_sp"][G % 4] = exp_s
                            mw = 0
                            if kt >= 4 * qc:
                                for hh in range(2):
                                    mw = r.op("vector", lambda e, ptsl=ptsl, hh=hh,
                                              off=off: e.tensor_tensor(
                                        out=PT[ptsl][:, hh * 512 + off: hh * 512 + off + P],
                                        in0=PT[ptsl][:, hh * 512 + off: hh * 512 + off + P],
                                        in1=trib[:, :], op=ALU.mult),
                                        waits=[("S", exp_s), ("V", c_tri)])
                            if pend is not None:
                                emit_pv(*pend)
                            pend = (kt, off, ptsl, exp_s, mw)
                        emit_pv(*pend)
                        rc0 = r.op("vector", lambda e: e.tensor_copy(
                            out=rrb[0:1, 0:512], in_=pPV[0][64:65, :]),
                            waits=[("T", tpv)])
                        rc1 = r.op("vector", lambda e: e.tensor_copy(
                            out=rrb[0:1, 512:1024], in_=pPV[1][64:65, :]),
                            waits=[("V", rc0)])
                        nbk = pt_bank()
                        tb0 = r.op("tensor", lambda e, b=nbk: e.matmul(
                            pT[0:64, b * 512:(b + 1) * 512], onesr_b[0:1, 0:64],
                            rrb[0:1, 0:512], start=True, stop=True),
                            waits=[("V", rc1), ptrd[nbk]])
                        tb1 = r.op("tensor", lambda e, b=nbk: e.matmul(
                            pT[64:128, b * 512:(b + 1) * 512], onesr_b[0:1, 0:64],
                            rrb[0:1, 512:1024],
                            start=True, stop=True, tile_position=(0, 64)),
                            waits=[("V", rc1)])
                        cpw = r.op("vector", lambda e, b=nbk: e.reciprocal(
                            out=rbsb[:, :], in_=pT[:, b * 512:(b + 1) * 512]),
                            waits=[("T", tb1)])
                        ptwr[nbk] = tb1
                        ptrd[nbk] = ("V", cpw)
                        for hh in range(2):
                            nm[hh] = r.op("vector", lambda e, hh=hh, hp=hp:
                                          e.tensor_tensor(
                                out=ost[64 * hh:64 * hh + 64, hp * 512:(hp + 1) * 512],
                                in0=pPV[hh][0:64, :],
                                in1=rbsb[64 * hh:64 * hh + 64, :], op=ALU.mult),
                                waits=[("V", cpw), ("T", last["ost_t"])])
                            last["pPV_free"][hh] = nm[hh]
                        for _ in range(2):
                            if pending_dq:
                                pending_dq.pop(0)()
                    tpr = 0
                    for ttg in range(2):
                        evp = 0
                        for tt2 in range(2):
                            tt = ttg * 2 + tt2
                            for half in range(2):
                                qq = pt_bank()
                                psp = pT[:, qq * 512: qq * 512 + 512]
                                for pp in range(2):
                                    tw = [("V", nm[0]), ("V", nm[1]), ("Dwp", gw)]
                                    if pp == 0:
                                        tw.append(ptrd[qq])
                                    tpr = r.op("tensor", lambda e, tt=tt, half=half,
                                               pp=pp, ps=psp: e.matmul(
                                        ps, ost[:, pp * 512 + tt * P: pp * 512 + (tt + 1) * P],
                                        wpsb[:, pp * DM + half * 512: pp * DM + (half + 1) * 512],
                                        start=(pp == 0), stop=(pp == 1)), waits=tw)
                                evp = r.op("vector", lambda e, tt2=tt2, half=half,
                                           ps=psp: e.tensor_copy(
                                    out=prsb[:, tt2 * DM + half * 512: tt2 * DM + (half + 1) * 512],
                                    in_=ps), waits=[("T", tpr), ("Drin", last["prsb_d"])])
                                ptwr[qq] = tpr
                                ptrd[qq] = ("V", evp)
                        rrow = qc * 512 + ttg * 256
                        dpr = r.op("sync", lambda e, rrow=rrow: e.dma_start(
                            out=rin.ap()[rrow: rrow + 256,
                                         :].rearrange("(a p) c -> p a c", p=P),
                            in_=prsb[:, :].rearrange("p (a c) -> p a c", a=2)),
                            waits=[("V", evp)], sem="Drin")
                        last["prsb_d"] = dpr
                    last["ost_t"] = tpr
                if G == NG // 2 - 1:
                    r.op("gpsimd", lambda e: e.collective_compute(
                        "ReduceScatter", ALU.add,
                        replica_groups=[[0, 1, 2, 3], [4, 5, 6, 7]],
                        ins=[rin.ap()[0:T // 2, :].opt()],
                        outs=[rsout.ap()[0:TQ // 2, :].opt()]),
                        waits=[("Drin", last["prsb_d"])], sem="CC")
            for u in pending_dq:
                u()
            pending_dq = []
            if last["dq_pend"] is not None:
                _ci = last["dq_pend"]
                last["dq_pend"] = None
                nw1f = 4 * NDP
                gq, gs = last["dq_ld"]
                if _ci >= nw1f:
                    g2 = _ci - nw1f
                    dqf = r.op("vector", lambda e: e.scalar_tensor_tensor(
                        out=w1bf[:, 0:DM], in0=q1f[:, 0:DM], scalar=-128.0,
                        in1=s1bf[:, 0:DM], op0=ALU.add, op1=ALU.mult),
                        waits=[("Dq", gq), ("Dsc", gs), ("Dqo", last["dq_go"])])
                    gof = r.op("gpsimd", lambda e, g2=g2: e.dma_start(
                        out=w2s.ap()[g2 * P:(g2 + 1) * P, :], in_=w1bf[:, 0:DM]),
                        waits=[("V", dqf)], sem="Dqo")
                    last["dq_v"] = dqf
                    last["dq_go"] = gof

            bfin = last["b1_v"]

            cc = r.op("gpsimd", lambda e: e.collective_compute(
                "ReduceScatter", ALU.add,
                replica_groups=[[0, 1, 2, 3], [4, 5, 6, 7]],
                ins=[rin.ap()[T // 2: T, :].opt()],
                outs=[rsout.ap()[TQ // 2: TQ, :].opt()]),
                waits=[("Drin", last["prsb_d"])], sem="CC")

            PH = [(s, r.n[s]) for s in ("T", "V", "S", "Dpro", "Dwp", "Dx", "Dx2",
                                        "Dsc", "Drin", "Dq", "Dqo")]
            rs_wait = r.n["CC"]
            pro_d = r.n["Dpro"]
            emit_block()

        with contextlib.ExitStack() as dd:
            de = dd.enter_context
            xsd = [de(nc.sbuf_tensor(f"xsd{i}", [P, 4 * DM], F32)) for i in range(2)]
            xnd = [de(nc.sbuf_tensor(f"xnd{i}", [P, 4 * DM], F32)) for i in range(2)]
            hb = [de(nc.sbuf_tensor(f"hb{i}", [P, 4 * DM], BF16)) for i in range(2)]
            hT = [de(nc.sbuf_tensor(f"hT{i}", [P, NDP * 512], BF16)) for i in range(2)]
            yT = de(nc.sbuf_tensor("yT", [P, NFP * 512], BF16))
            zT = de(nc.sbuf_tensor("zT", [P, NDP * 512], BF16))
            w1sb = [de(nc.sbuf_tensor(f"w1sb{i}", [P, NDP * 512], BF16)) for i in range(2)]
            w2sb = [de(nc.sbuf_tensor(f"w2sb{i}", [P, 2 * DM], BF16)) for i in range(2)]
            rsq = [de(nc.sbuf_tensor(f"rsq{i}", [P, 4 * DM], F32)) for i in range(2)]
            bpbt = de(nc.sbuf_tensor("bpbt", [P, DM], F32))
            etm = de(nc.sbuf_tensor("etm", [P, DM], F32))
            ost2 = de(nc.sbuf_tensor("ost2", [P, DM], F32))
            lnst2 = de(nc.sbuf_tensor("lnst2", [P, 16], F32))
            pD = de(nc.psum_tensor("pD", [P, 3072], F32))
            pT2 = de(nc.psum_tensor("pT2", [P, 2048], BF16))

            def ln2_tile(waits0, xsap, outap):
                w0 = r.op("vector", lambda e: e.bn_stats(
                    out=lnst2[:, 0:6], in_=xsap[:, 0:512]), waits=waits0)
                w1 = r.op("vector", lambda e: e.bn_stats(
                    out=lnst2[:, 6:12], in_=xsap[:, 512:1024]), waits=[("V", w0)])
                w2 = r.op("vector", lambda e: e.bn_aggr(
                    out=lnst2[:, 12:14], in_=lnst2[:, 0:12]), waits=[("V", w1)])
                s0 = r.op("scalar", lambda e: e.activation(
                    out=lnst2[:, 14:15], in_=lnst2[:, 13:14], func=AF.Ln, bias=EPS,
                    scale=1.0), waits=[("V", w2)])
                s1_ = r.op("scalar", lambda e: e.activation(
                    out=lnst2[:, 15:16], in_=lnst2[:, 14:15], func=AF.Exp, bias=0.0,
                    scale=-0.5), waits=[("S", s0)])
                return r.op("vector", lambda e: e.tensor_scalar(
                    out=outap, in0=xsap, scalar1=lnst2[:, 12:13],
                    scalar2=lnst2[:, 15:16], op0=ALU.subtract, op1=ALU.mult),
                    waits=[("S", s1_)])

            dbp = r.op("sync", lambda e: e.dma_start(
                out=bpbt[:, :], in_=bass.AP(ap_bpb.tensor, 0, [[0, P], [1, DM]])),
                waits=PH, sem="Dpro")
            dl = dict(hb_t=[0, 0], hT_rd=[0, 0], yT_t=0, zT_t=0, w1_t=[0, 0],
                      w2_t=[0, 0], etm_v=0, rsq_v=[0, 0],
                      pd1_s=[0, 0], pd2_v=[0] * 4, xn_ep=[0, 0],
                      out_d=0, pt2_v=[0] * 16, pt2_seq=0,
                      pt2_bank=[0, 0])

            ST = {}

            def front_a(s):
                sl = s % 2
                dxsem = "Dxa" if sl == 0 else "Dxb"
                dx = r.op("sync", lambda e, s=s: e.dma_start(
                    out=xsd[s % 2][:, :].rearrange("p (a c) -> p a c", a=4),
                    in_=ap_xq[s * 512:(s + 1) * 512, :].rearrange(
                        "(a p) c -> p a c", p=P)),
                    waits=PH + [("V", dl["xn_ep"][sl])], sem=dxsem)
                cc_need = 1 if (s + 1) * 512 <= TQ // 2 else 2
                drs = r.op("gpsimd", lambda e, s=s: e.dma_start(
                    out=rsq[sl][:, :].rearrange("p (a c) -> p a c", a=4),
                    in_=rsout.ap()[s * 512:(s + 1) * 512, :].rearrange(
                        "(a p) c -> p a c", p=P)),
                    waits=[("CC", cc_need), ("V", dl["rsq_v"][sl])], sem="Drs")
                wv = 0
                for tt in range(4):
                    wv = ln2_tile([(dxsem, dx)], xsd[sl][:, tt * DM:(tt + 1) * DM],
                                  xnd[sl][:, tt * DM:(tt + 1) * DM])
                    wv = ln2_tile([("V", wv), ("T", dl["hb_t"][sl])],
                                  xnd[sl][:, tt * DM:(tt + 1) * DM],
                                  hb[sl][:, tt * DM:(tt + 1) * DM])
                ST[s] = dict(sl=sl, dx=dx, dxsem=dxsem, drs=drs, wv=wv)

            def front_b(s):
                sl = ST[s]["sl"]
                hb_v = ST[s]["wv"]
                cp = 0
                t1 = 0
                for tt in range(4):
                    for dp in range(NDP):
                        i = dl["pt2_seq"]
                        k = (i % 2) * 8 + ((i // 2) % 8)
                        dl["pt2_seq"] = i + 1
                        t1 = r.op("tensor", lambda e, tt=tt, dp=dp, k=k, sl=sl:
                                  e.transpose(
                            pT2[:, k * P:(k + 1) * P],
                            hb[sl][:, tt * DM + dp * P: tt * DM + (dp + 1) * P],
                            idb[:, :]),
                            waits=[("V", hb_v), ("V", dl["pt2_v"][k]),
                                   ("V", dl["pt2_bank"][k // 8]), ("V", c_id)])
                        cp = r.op("vector", lambda e, tt=tt, dp=dp, k=k, sl=sl:
                                  e.tensor_copy(
                            out=hT[sl][:, dp * 512 + tt * P: dp * 512 + (tt + 1) * P],
                            in_=pT2[:, k * P:(k + 1) * P]),
                            waits=[("T", t1), ("T", dl["hT_rd"][sl])])
                        dl["pt2_v"][k] = cp
                        dl["pt2_bank"][k // 8] = cp
                ST[s]["hT_v"] = cp
                dl["hb_t"][sl] = t1

            def do_fc1(s):
                sl = ST[s]["sl"]
                hT_v = ST[s]["hT_v"]
                sy = 0
                tm = 0
                for fc in range(8):
                    wl = fc % 2
                    wsem = "Dw1a" if wl == 0 else "Dw1b"
                    dw = r.op("sync", lambda e, fc=fc, w=wl: e.dma_start(
                        out=w1sb[w][:, :].rearrange("p (a c) -> p a c", a=NDP),
                        in_=w1s.ap()[:, fc * 512:(fc + 1) * 512].rearrange(
                            "(dp p) f -> p dp f", p=P)),
                        waits=PH + [("T", dl["w1_t"][wl])], sem=wsem)
                    for fb in range(4):
                        fba = fc * 4 + fb
                        pq = fba % 2
                        psl = pD[:, 2048 + pq * 512: 2048 + (pq + 1) * 512]
                        for dp in range(NDP):
                            tw = [(wsem, dw), ("V", hT_v)]
                            if dp == 0:
                                tw.append(("S", dl["pd1_s"][pq]))
                            tm = r.op("tensor", lambda e, fb=fb, dp=dp, w=wl,
                                      ps=psl, sl=sl: e.matmul(
                                ps, w1sb[w][:, dp * 512 + fb * P: dp * 512 + (fb + 1) * P],
                                hT[sl][:, dp * 512:(dp + 1) * 512],
                                start=(dp == 0), stop=(dp == NDP - 1)), waits=tw)
                        sy = r.op("scalar", lambda e, fba=fba, ps=psl: e.activation(
                            out=yT[:, fba * 512:(fba + 1) * 512], in_=ps,
                            func=AF.Silu, bias=bias1[:, fba:fba + 1], scale=1.0),
                            waits=[("T", tm), ("T", dl["yT_t"]), ("V", bfin)])
                        dl["pd1_s"][pq] = sy
                    dl["w1_t"][wl] = tm
                dl["hT_rd"][sl] = tm
                ST[s]["yT_s"] = sy

            def do_fc2(s):
                yT_s = ST[s]["yT_s"]
                ev = 0
                tm2 = 0
                for wave in range(2):
                    for fpg in range(NFP // 2):
                        w2l = fpg % 2
                        w2sem = "Dw2a" if w2l == 0 else "Dw2b"
                        dw2 = r.op("sync", lambda e, fpg=fpg, w=w2l: e.dma_start(
                            out=w2sb[w][:, :].rearrange("p (a c) -> p a c", a=2),
                            in_=w2s.ap()[fpg * 2 * P:(fpg + 1) * 2 * P, :].rearrange(
                                "(a p) c -> p a c", p=P)),
                            waits=PH + [("T", dl["w2_t"][w2l])], sem=w2sem)
                        for fpl in range(2):
                            fp = fpg * 2 + fpl
                            for ob in range(4):
                                oba = wave * 4 + ob
                                tw = [(w2sem, dw2), ("S", yT_s)]
                                if fp == 0:
                                    tw.append(("V", dl["pd2_v"][ob]))
                                tm2 = r.op("tensor", lambda e, ob=ob, oba=oba, w=w2l,
                                           fp=fp, fpl=fpl: e.matmul(
                                    pD[:, ob * 512:(ob + 1) * 512],
                                    w2sb[w][:, fpl * DM + oba * P: fpl * DM + (oba + 1) * P],
                                    yT[:, fp * 512:(fp + 1) * 512],
                                    start=(fp == 0), stop=(fp == NFP - 1)), waits=tw)
                        dl["w2_t"][w2l] = tm2
                    for ob in range(4):
                        oba = wave * 4 + ob
                        ev = r.op("vector", lambda e, ob=ob, oba=oba: e.tensor_scalar(
                            out=zT[:, oba * 512:(oba + 1) * 512],
                            in0=pD[:, ob * 512:(ob + 1) * 512],
                            scalar1=b2t_sb[:, oba:oba + 1], scalar2=None,
                            op0=ALU.add),
                            waits=[("T", tm2), ("T", dl["zT_t"]), ("Dpro", pro_d)])
                        dl["pd2_v"][ob] = ev
                dl["yT_t"] = tm2
                ST[s]["zT_v"] = ev

            def do_epi(s):
                sl = ST[s]["sl"]
                wv = ST[s]["wv"]
                drs = ST[s]["drs"]
                zT_v = ST[s]["zT_v"]
                e4 = 0
                e2 = 0
                for tt in range(4):
                    bnk = tt % 2
                    ztr = 0
                    for ob in range(NDP):
                        ztr = r.op("tensor", lambda e, ob=ob, tt=tt, bnk=bnk:
                                   e.transpose(
                            pT2[:, bnk * 1024 + ob * P: bnk * 1024 + (ob + 1) * P],
                            zT[:, ob * 512 + tt * P: ob * 512 + (tt + 1) * P],
                            idb[:, :]),
                            waits=[("V", zT_v), ("V", dl["pt2_bank"][bnk])])
                    dl["zT_t"] = ztr
                    e1 = r.op("vector", lambda e, tt=tt, sl=sl: e.tensor_tensor(
                        out=etm[:, :], in0=xsd[sl][:, tt * DM:(tt + 1) * DM],
                        in1=xnd[sl][:, tt * DM:(tt + 1) * DM], op=ALU.add),
                        waits=[("V", wv)])
                    e2 = r.op("vector", lambda e, tt=tt, sl=sl: e.tensor_tensor(
                        out=etm[:, :], in0=etm[:, :],
                        in1=rsq[sl][:, tt * DM:(tt + 1) * DM], op=ALU.add),
                        waits=[("V", e1), ("Drs", drs)])
                    e3 = r.op("vector", lambda e: e.tensor_tensor(
                        out=etm[:, :], in0=etm[:, :], in1=bpbt[:, :], op=ALU.add),
                        waits=[("V", e2), ("Dpro", dbp)])
                    e4 = r.op("vector", lambda e, tt=tt, bnk=bnk:
                              e.scalar_tensor_tensor(
                        out=ost2[:, :],
                        in0=pT2[:, bnk * 1024:(bnk + 1) * 1024],
                        scalar=0.5, in1=etm[:, :], op0=ALU.mult, op1=ALU.add),
                        waits=[("V", e3), ("T", ztr), ("Dout", dl["out_d"])])
                    dl["etm_v"] = e4
                    dl["pt2_bank"][bnk] = e4
                    for kk in range(8):
                        dl["pt2_v"][bnk * 8 + kk] = e4
                    do = r.op("sync", lambda e, s=s, tt=tt: e.dma_start(
                        out=ap_out[s * 512 + tt * P: s * 512 + (tt + 1) * P, :],
                        in_=ost2[:, :]),
                        waits=[("V", e4)], sem="Dout")
                    dl["out_d"] = do
                dl["xn_ep"][sl] = e4
                dl["rsq_v"][sl] = e2

            front_a(0)
            front_b(0)
            for s in range(NST):
                if s + 1 < NST:
                    front_a(s + 1)
                do_fc1(s)
                if s > 0:
                    do_epi(s - 1)
                if s + 1 < NST:
                    front_b(s + 1)
                do_fc2(s)
            do_epi(NST - 1)

            fin = r.n["Dout"]
            r.op("sync", lambda e: e.nop(), waits=[("Dout", fin)], sem="T")
            emit_block()

    return nc


def kernel(**inputs):
    T = 8192
    TQ = T // 4
    x = np.asarray(inputs["L0_attn_x"])
    args = (x, np.asarray(inputs["L0_attn_w_qkv"]), np.asarray(inputs["L0_attn_b_qkv"]),
            np.asarray(inputs["L0_attn_w_proj"]), np.asarray(inputs["L0_attn_b_proj"]),
            np.asarray(inputs["L0_ffn_ln_gamma"]), np.asarray(inputs["L0_ffn_ln_beta"]),
            np.asarray(inputs["L0_ffn_fc1_q"]), np.asarray(inputs["L0_ffn_fc1_scales"]),
            np.asarray(inputs["L0_ffn_fc1_bias"]), np.asarray(inputs["L0_ffn_fc2_q"]),
            np.asarray(inputs["L0_ffn_fc2_scales"]), np.asarray(inputs["L0_ffn_fc2_bias"]))
    nc = build(T)
    in_maps = [make_core_inputs(c, T, *args) for c in range(8)]
    res = run_bass_kernel_spmd(nc, in_maps, core_ids=list(range(8)))
    out = np.empty((2, T, DM), dtype=np.float32)
    for c in range(8):
        b, hg = c // 4, c % 4
        oq = np.asarray(res.results[c]["out_q"], dtype=np.float32)
        out[b, hg * T // 8:(hg + 1) * T // 8, :] = oq[0:TQ // 2]
        out[b, T // 2 + hg * T // 8: T // 2 + (hg + 1) * T // 8, :] = oq[TQ // 2:]
    return out


def make_core_inputs(c, T, x, wqkv, bqkv, wproj, bproj, gam, bet,
                     q1, s1, b1, q2, s2, b2):
    TQ = T // 4
    b, hg = c // 4, c % 4
    cs = slice(QK * hg, QK * (hg + 1))
    return dict(
        xb=np.ascontiguousarray(x[b]),
        xq=np.ascontiguousarray(np.concatenate(
            [x[b, hg * T // 8:(hg + 1) * T // 8, :],
             x[b, T // 2 + hg * T // 8: T // 2 + (hg + 1) * T // 8, :]], axis=0)),
        wq=np.ascontiguousarray(wqkv[:, 0:1024][:, cs]),
        wk=np.ascontiguousarray(wqkv[:, 1024:2048][:, cs]),
        wv=np.ascontiguousarray(wqkv[:, 2048:3072][:, cs]),
        bq=np.ascontiguousarray(bqkv[0:1024][cs].reshape(2, P).T),
        bk=np.ascontiguousarray(bqkv[1024:2048][cs].reshape(2, P).T),
        bv=np.ascontiguousarray(bqkv[2048:3072][cs].reshape(2, P).T),
        wp=np.ascontiguousarray(wproj[cs, :]),
        bpb=np.ascontiguousarray(bproj.reshape(1, DM)),
        gam=np.ascontiguousarray(gam.reshape(NDP, P).T),
        bet=np.ascontiguousarray(bet.reshape(NDP, P).T),
        q1=np.ascontiguousarray(q1),
        s1=np.ascontiguousarray(s1),
        b1t=np.ascontiguousarray(b1.reshape(NFP, P).T),
        q2=np.ascontiguousarray(q2),
        s2=np.ascontiguousarray(s2),
        b2t=np.ascontiguousarray(b2.reshape(NDP, P).T),
        idf=np.eye(P, dtype=np.float32),
        onesr=np.ones((1, P), dtype=np.float32),
        onesc=np.ones((P, 1), dtype=np.float32),
        trim=np.where(np.arange(P)[:, None] <= np.arange(P)[None, :],
                      1.0, 0.0).astype(np.float32),
        cvals=np.array([[EPS, -0.5, SCALE, 1.0, 0.0, 0.0, 0.0, 0.0]],
                       dtype=np.float32),
    )


# version 17
# speedup vs baseline: 1.0660x; 1.0660x over previous
import contextlib
import numpy as np

import concourse.bass as bass
import concourse.mybir as mybir
from concourse.bass_utils import run_bass_kernel_spmd

F32 = mybir.dt.float32
BF16 = mybir.dt.bfloat16
I32 = mybir.dt.int32
AF = mybir.ActivationFunctionType
ALU = mybir.AluOpType
AX = mybir.AxisListType

P = 128
DM = 1024
NDP = DM // P
HD = 64
NHC = 4
QK = NHC * HD
F1 = 4096
NFP = F1 // P
EPS = 1e-5
SCALE = 0.125

ENGS = ("tensor", "vector", "scalar", "sync", "gpsimd")
SEM_OF = dict(tensor="T", vector="V", scalar="S")
DMA_SEMS = ("Dpro", "Dwp", "Dx", "Dx2", "Dsc", "Drin", "Dq", "Dqo",
            "Dw1a", "Dw1b", "Dw2a", "Dw2b", "Dxa", "Dxb", "Drs", "Dout")
ALL_SEMS = ("T", "V", "S", "Gp", "CC") + DMA_SEMS


class Rec:
    def __init__(self):
        self.q = {e: [] for e in ENGS}
        self.n = {s: 0 for s in ALL_SEMS}

    def op(self, eng, fn, waits=(), sem=None):
        s = sem or SEM_OF[eng]
        a = 16 if s.startswith("D") else 1
        if eng == "gpsimd" and s == "Gp":
            pass
        waits = list(waits)
        if eng == "vector":
            waits.append(("V", self.n["V"]))
        elif eng == "scalar":
            waits.append(("S", self.n["S"]))
        self.n[s] += a
        w = tuple((ws, wv) for ws, wv in waits if wv > 0)
        self.q[eng].append((fn, w, s, a))
        return self.n[s]


def t_of(h):
    return h[:, :].tensor


def build(T):
    NG = T // 1024
    NKT = T // 128
    TQ = T // 4
    NST = TQ // 512
    assert NST >= 1 and T % 1024 == 0

    nc = bass.Bass("TRN2", target_bir_lowering=False, debug=False)
    ap_xb = nc.dram_tensor("xb", [T, DM], F32, kind="ExternalInput").ap()
    ap_xq = nc.dram_tensor("xq", [TQ, DM], F32, kind="ExternalInput").ap()
    ap_wq = nc.dram_tensor("wq", [DM, QK], F32, kind="ExternalInput").ap()
    ap_wk = nc.dram_tensor("wk", [DM, QK], F32, kind="ExternalInput").ap()
    ap_wv = nc.dram_tensor("wv", [DM, QK], F32, kind="ExternalInput").ap()
    ap_bq = nc.dram_tensor("bq", [P, 2], F32, kind="ExternalInput").ap()
    ap_bk = nc.dram_tensor("bk", [P, 2], F32, kind="ExternalInput").ap()
    ap_bv = nc.dram_tensor("bv", [P, 2], F32, kind="ExternalInput").ap()
    ap_wp = nc.dram_tensor("wp", [QK, DM], F32, kind="ExternalInput").ap()
    ap_bpb = nc.dram_tensor("bpb", [1, DM], F32, kind="ExternalInput").ap()
    ap_gam = nc.dram_tensor("gam", [P, NDP], F32, kind="ExternalInput").ap()
    ap_bet = nc.dram_tensor("bet", [P, NDP], F32, kind="ExternalInput").ap()
    ap_q1 = nc.dram_tensor("q1", [DM, F1], I32, kind="ExternalInput").ap()
    ap_s1 = nc.dram_tensor("s1", [NDP, F1], F32, kind="ExternalInput").ap()
    ap_b1t = nc.dram_tensor("b1t", [P, NFP], F32, kind="ExternalInput").ap()
    ap_q2 = nc.dram_tensor("q2", [F1, DM], I32, kind="ExternalInput").ap()
    ap_s2 = nc.dram_tensor("s2", [NFP, DM], F32, kind="ExternalInput").ap()
    ap_b2t = nc.dram_tensor("b2t", [P, NDP], F32, kind="ExternalInput").ap()
    ap_idf = nc.dram_tensor("idf", [P, P], F32, kind="ExternalInput").ap()
    ap_onesr = nc.dram_tensor("onesr", [1, P], F32, kind="ExternalInput").ap()
    ap_onesc = nc.dram_tensor("onesc", [P, 1], F32, kind="ExternalInput").ap()
    ap_trim = nc.dram_tensor("trim", [P, P], F32, kind="ExternalInput").ap()
    ap_cv = nc.dram_tensor("cvals", [1, 8], F32, kind="ExternalInput").ap()
    ap_out = nc.dram_tensor("out_q", [TQ, DM], F32, kind="ExternalOutput").ap()

    rin = nc.dram_tensor("rin", [T, DM], F32)
    rsout = nc.dram_tensor("rsout", [TQ, DM], F32)
    w1s = nc.dram_tensor("w1s", [DM, F1], BF16)
    w2s = nc.dram_tensor("w2s", [F1, DM], BF16)

    r = Rec()

    with contextlib.ExitStack() as perm:
        pe = perm.enter_context
        sems = {s: pe(nc.semaphore("sem" + s)) for s in ALL_SEMS}

        def emit_block():
            blk = nc.Block()
            with blk as block:
                def mk(eng_name):
                    def _replay(e):
                        for fn, waits, s, a in r.q[eng_name]:
                            for ws, wv in waits:
                                e.wait_ge(sems[ws], wv)
                            fn(e).then_inc(sems[s], a)
                    return _replay
                block.tensor(mk("tensor"))
                block.vector(mk("vector"))
                block.scalar(mk("scalar"))
                block.sync(mk("sync"))
                block.gpsimd(mk("gpsimd"))
            for e in ENGS:
                r.q[e] = []

        idf_sb = pe(nc.sbuf_tensor("idf_sb", [P, P], F32))
        idb = pe(nc.sbuf_tensor("idb", [P, P], BF16))
        onesr_b = pe(nc.sbuf_tensor("onesr_b", [1, P], BF16))
        onesc_b = pe(nc.sbuf_tensor("onesc_b", [P, 1], BF16))
        trib = pe(nc.sbuf_tensor("trib", [P, P], BF16))
        stag = pe(nc.sbuf_tensor("stag", [P, 2 * P + 2], F32))
        bq_sb = pe(nc.sbuf_tensor("bq_sb", [P, 2], F32))
        bk_sb = pe(nc.sbuf_tensor("bk_sb", [P, 2], F32))
        bv_sb = pe(nc.sbuf_tensor("bv_sb", [P, 2], F32))
        gam_sb = pe(nc.sbuf_tensor("gam_sb", [P, NDP], F32))
        bet_sb = pe(nc.sbuf_tensor("bet_sb", [P, NDP], F32))
        betb = pe(nc.sbuf_tensor("betb", [P, NDP], BF16))
        b1t_sb = pe(nc.sbuf_tensor("b1t_sb", [P, NFP], F32))
        bias1 = pe(nc.sbuf_tensor("bias1", [P, NFP], F32))
        b2t_sb = pe(nc.sbuf_tensor("b2t_sb", [P, NDP], F32))
        biasM = pe(nc.sbuf_tensor("biasM", [P, 8], F32))
        statr = pe(nc.sbuf_tensor("statr", [1, 16], F32))
        ctile = pe(nc.sbuf_tensor("ctile", [P, 8], F32))
        for i, val in enumerate((EPS, -0.5, SCALE, 1.0, 0.0)):
            nc.const_aps.register(F32, val, ctile[:, i:i + 1])

        with contextlib.ExitStack() as sw:
            se = sw.enter_context
            QT = se(nc.sbuf_tensor("QT", [P, 2 * T], BF16))
            KT = se(nc.sbuf_tensor("KT", [P, 2 * T], BF16))
            VS = se(nc.sbuf_tensor("VS", [P, NKT * 260], BF16))
            ost = se(nc.sbuf_tensor("ost", [P, 1024], BF16))
            xs = [se(nc.sbuf_tensor(f"xs{i}", [P, 4 * DM], F32)) for i in range(2)]
            xnb = [se(nc.sbuf_tensor(f"xnb{i}", [P, 2 * DM], F32)) for i in range(2)]
            xnT = se(nc.sbuf_tensor("xnT", [P, NDP * 512], BF16))
            vts = se(nc.sbuf_tensor("vts", [P, 1024], F32))
            PT = [se(nc.sbuf_tensor(f"PT{i}", [P, 1024], BF16)) for i in range(2)]
            sqb = se(nc.sbuf_tensor("sqb", [P, 512], BF16))
            wsb = se(nc.sbuf_tensor("wsb", [P, NDP * 768], BF16))
            wpsb = se(nc.sbuf_tensor("wpsb", [P, 2 * DM], BF16))
            prsb = se(nc.sbuf_tensor("prsb", [P, 2 * DM], F32))
            lnst = se(nc.sbuf_tensor("lnst", [P, 80], F32))
            rrb = se(nc.sbuf_tensor("rrb", [1, 1024], BF16))
            rbsb = se(nc.sbuf_tensor("rbsb", [P, 512], F32))
            q1f = se(nc.sbuf_tensor("q1f", [P, 1024], F32))
            s1bf = se(nc.sbuf_tensor("s1bf", [P, 1024], F32))
            w1bf = se(nc.sbuf_tensor("w1bf", [P, 1024], BF16))
            mrow = se(nc.sbuf_tensor("mrow", [1, 4], BF16))

            pS = se(nc.psum_tensor("pS", [P, 2048], F32))
            pPV = [se(nc.psum_tensor(f"pPV{i}", [P, 512], F32)) for i in range(2)]
            pT = se(nc.psum_tensor("pT", [P, 1024], F32))

            r.op("sync", lambda e: e.dma_start(
                out=ctile[:, :], in_=bass.AP(ap_cv.tensor, 0, [[0, P], [1, 8]])),
                sem="Dpro")
            for src, dst in ((ap_bq, bq_sb), (ap_bk, bk_sb), (ap_bv, bv_sb),
                             (ap_gam, gam_sb), (ap_bet, bet_sb),
                             (ap_b1t, b1t_sb), (ap_b2t, b2t_sb)):
                r.op("sync", lambda e, s=src, d=dst: e.dma_start(out=d[:, :], in_=s),
                     sem="Dpro")
            r.op("sync", lambda e: e.dma_start(out=idf_sb[:, :], in_=ap_idf), sem="Dpro")
            r.op("sync", lambda e: e.dma_start(out=stag[:, 0:P], in_=ap_trim), sem="Dpro")
            r.op("sync", lambda e: e.dma_start(out=stag[0:1, P:2 * P], in_=ap_onesr),
                 sem="Dpro")
            pro = r.op("sync", lambda e: e.dma_start(
                out=stag[:, 2 * P:2 * P + 1], in_=ap_onesc), sem="Dpro")
            r.op("scalar", lambda e: e.nop(), waits=[("Dpro", pro)])
            c_id = r.op("vector", lambda e: e.tensor_copy(out=idb[:, :], in_=idf_sb[:, :]),
                        waits=[("Dpro", pro)])
            c_or = r.op("vector", lambda e: e.tensor_copy(out=onesr_b[:, :],
                                                          in_=stag[0:1, P:2 * P]))
            c_oc = r.op("vector", lambda e: e.tensor_copy(out=onesc_b[:, :],
                                                          in_=stag[:, 2 * P:2 * P + 1]))
            c_tri = r.op("vector", lambda e: e.tensor_copy(out=trib[:, :],
                                                           in_=stag[:, 0:P]))
            c_bet = r.op("vector", lambda e: e.tensor_copy(out=betb[:, :],
                                                           in_=bet_sb[:, :]))
            for src, cb0 in ((ap_wq, 0), (ap_wk, 2), (ap_wv, 4)):
                dst = wsb[:, :].rearrange("p (dp c) -> p dp c", dp=NDP)[
                    :, :, cb0 * P: cb0 * P + 256]
                r.op("gpsimd", lambda e, s=src, d=dst: e.dma_start(
                    out=d, in_=s.rearrange("(dp p) c -> p dp c", p=P)), sem="Dwp")
            gw = r.op("gpsimd", lambda e: e.dma_start(
                out=wpsb[:, :].rearrange("p (a c) -> p a c", a=2),
                in_=ap_wp.rearrange("(a p) c -> p a c", p=P)), sem="Dwp")
            vm = 0
            for h in range(NHC):
                vm = r.op("vector", lambda e, h=h: e.memset(
                    VS[:, :].rearrange("p (kt s) -> p kt s", kt=NKT)[
                        :, :, 65 * h + 64: 65 * h + 65], 1.0))
            z0 = r.op("vector", lambda e: e.memset(statr[0:1, :], 0.0))
            z1 = r.op("vector", lambda e: e.tensor_copy(
                out=bias1[:, :], in_=b1t_sb[:, :]), waits=[("Dpro", 16 * 12)])

            last = dict(
                xs_v=[0, 0], xnT_t=0, xnb_rd=[0, 0], vs_copy=0, pt_v=[0] * 16, pt_seq=0,
                pt_prev=[0, 0], pS_exp_sl=[0, 0],
                pS_free=[0] * 4, pS_exp=0, pW_free=0, pPV_free=[0, 0],
                PT_t=[0, 0], sq_t=0, ost_t=0, prsb_d=0,
                dq_v=0, dq_go=0, dq_pend=None, dq_ld=(0, 0), b1_v=z1, stat_v=z0,
                biasM_v=0, biasM_s=0,
            )

            def ln_stats(waits0, xsap, c0):
                w0 = r.op("vector", lambda e, c0=c0: e.bn_stats(
                    out=lnst[:, c0:c0 + 6], in_=xsap[:, 0:512]), waits=waits0)
                w1 = r.op("vector", lambda e, c0=c0: e.bn_stats(
                    out=lnst[:, c0 + 6:c0 + 12], in_=xsap[:, 512:1024]),
                    waits=[("V", w0)])
                return r.op("vector", lambda e, c0=c0: e.bn_aggr(
                    out=lnst[:, c0 + 12:c0 + 14], in_=lnst[:, c0:c0 + 12]),
                    waits=[("V", w1)])

            def ln_fin(wstat, xsap, xnap, c0):
                s0 = r.op("scalar", lambda e, c0=c0: e.activation(
                    out=lnst[:, c0 + 14:c0 + 15], in_=lnst[:, c0 + 13:c0 + 14],
                    func=AF.Ln, bias=EPS, scale=1.0), waits=[("V", wstat)])
                s1_ = r.op("scalar", lambda e, c0=c0: e.activation(
                    out=lnst[:, c0 + 15:c0 + 16], in_=lnst[:, c0 + 14:c0 + 15],
                    func=AF.Exp, bias=0.0, scale=-0.5), waits=[("S", s0)])
                return r.op("vector", lambda e, c0=c0: e.tensor_scalar(
                    out=xnap, in0=xsap, scalar1=lnst[:, c0 + 12:c0 + 13],
                    scalar2=lnst[:, c0 + 15:c0 + 16], op0=ALU.subtract,
                    op1=ALU.mult), waits=[("S", s1_)])

            def ln_tile(waits0, xsap, xnap):
                w2 = ln_stats(waits0, xsap, 64)
                return ln_fin(w2, xsap, xnap, 64)

            ptwr = [0, 0]
            ptrd = [("V", 0), ("V", 0)]
            last["lnv"] = [0, 0, 0, 0]
            last["lnw"] = [0, 0, 0, 0]
            last["dx"] = ("Dx", 0)
            last["xnT_s"] = 0
            last["qkev3"] = 0
            last["biasM_vp"] = [0, 0, 0, 0]
            last["biasM_sp"] = [0, 0, 0, 0]
            prod_done = {}

            def pt_bank():
                b = last["pt_seq"] % 2
                last["pt_seq"] += 1
                return b

            def make_prod_units(G):
                units = []

                def U(f):
                    units.append(f)
                for sg in range(2):
                    base = 1024 * G + 512 * sg
                    slot = (2 * G + sg) % 2

                    def u_dma(base=base, slot=slot):
                        dxsem = "Dx" if slot == 0 else "Dx2"
                        dx = r.op("sync", lambda e, b=base, s=slot: e.dma_start(
                            out=xs[s][:, :].rearrange("p (a c) -> p a c", a=4),
                            in_=ap_xb[b:b + 512, :].rearrange(
                                "(a p) c -> p a c", p=P)),
                            waits=[("V", last["xs_v"][slot])], sem=dxsem)
                        last["dx"] = (dxsem, dx)
                    U(u_dma)
                    for tt in range(4):
                        def u_lns(tt=tt, slot=slot):
                            last["lnw"][tt] = ln_stats(
                                [last["dx"]],
                                xs[slot][:, tt * DM:(tt + 1) * DM], 16 * tt)
                        U(u_lns)
                    for tt in range(4):
                        def u_lnf(tt=tt, slot=slot):
                            hs = tt // 2
                            if tt % 2 == 0:
                                r.op("vector", lambda e: e.engine_nop(),
                                     waits=[("T", last["xnb_rd"][hs])])
                            last["lnv"][tt] = ln_fin(
                                last["lnw"][tt],
                                xs[slot][:, tt * DM:(tt + 1) * DM],
                                xnb[hs][:, (tt % 2) * DM:(tt % 2 + 1) * DM], 16 * tt)
                            if tt == 3:
                                last["xs_v"][slot] = last["lnv"][3]
                        U(u_lnf)
                    for tt in range(4):
                        for dpg in range(2):
                            def u_tr(tt=tt, dpg=dpg):
                                bnk = pt_bank()
                                t1 = 0
                                for dpl in range(4):
                                    dp = dpg * 4 + dpl
                                    tw = [("V", last["lnv"][tt]), ("Dpro", pro)]
                                    if dpl == 0:
                                        tw.append(ptrd[bnk])
                                    t1 = r.op("tensor", lambda e, tt=tt, dp=dp,
                                              bnk=bnk, dpl=dpl: e.transpose(
                                        pT[:, bnk * 512 + dpl * P: bnk * 512 + (dpl + 1) * P],
                                        xnb[tt // 2][:, (tt % 2) * DM + dp * P:
                                                      (tt % 2) * DM + (dp + 1) * P],
                                        idf_sb[:, :]), waits=tw)
                                cp = r.op("scalar", lambda e, tt=tt, dpg=dpg,
                                          bnk=bnk: e.copy(
                                    out=bass.AP(t_of(xnT), (dpg * 4) * 512 + tt * P,
                                                [[512 * NDP, P], [512, 4], [1, P]]),
                                    in_=pT[:, bnk * 512: bnk * 512 + 512]),
                                    waits=[("T", t1)])
                                ptwr[bnk] = t1
                                ptrd[bnk] = ("S", cp)
                                last["xnb_rd"][tt // 2] = t1
                                last["xnT_s"] = cp
                            U(u_tr)
                    for cb in range(6):
                        def u_mm(cb=cb, base=base):
                            bnk = pt_bank()
                            ps_reg = pT[:, bnk * 512: bnk * 512 + 512]
                            tm = 0
                            for dp in range(NDP):
                                tw = [("S", last["xnT_s"])]
                                if dp == 0:
                                    tw += [ptrd[bnk], ("Dwp", gw)]
                                tm = r.op("tensor", lambda e, cb=cb, dp=dp,
                                          ps=ps_reg: e.matmul(
                                    ps, wsb[:, dp * 768 + cb * P: dp * 768 + (cb + 1) * P],
                                    xnT[:, dp * 512:(dp + 1) * 512],
                                    start=(dp == 0), stop=(dp == NDP - 1)), waits=tw)
                            if cb < 2:
                                dst = QT[:, cb * T + base: cb * T + base + 512]
                                bap = bq_sb[:, cb:cb + 1]
                            elif cb < 4:
                                dst = KT[:, (cb - 2) * T + base: (cb - 2) * T + base + 512]
                                bap = bk_sb[:, cb - 2:cb - 1]
                            else:
                                dst = vts[:, (cb - 4) * 512:(cb - 4) * 512 + 512]
                                bap = bv_sb[:, cb - 4:cb - 3]
                            ev = r.op("vector", lambda e, d=dst, b=bap, ps=ps_reg:
                                      e.tensor_scalar(out=d, in0=ps, scalar1=b,
                                                      scalar2=None, op0=ALU.add),
                                      waits=[("T", tm), ("Dpro", pro)])
                            ptwr[bnk] = tm
                            ptrd[bnk] = ("V", ev)
                            last["xnT_t"] = tm
                            if cb == 3:
                                last["qkev3"] = ev
                            last["qkev_" + str(cb)] = ev
                        U(u_mm)
                    for vp in range(2):
                        def u_vt(vp=vp, sg=sg):
                            kt0 = G * 8 + sg * 4
                            bnk = pt_bank()
                            t1 = 0
                            for j in range(4):
                                tw = [("V", last["qkev_" + str(4 + vp)]),
                                      ("Dpro", pro)]
                                if j == 0:
                                    tw.append(ptrd[bnk])
                                t1 = r.op("tensor", lambda e, vp=vp, j=j, bnk=bnk:
                                          e.transpose(
                                    pT[:, bnk * 512 + j * P: bnk * 512 + (j + 1) * P],
                                    vts[:, vp * 512 + j * P: vp * 512 + (j + 1) * P],
                                    idf_sb[:, :]), waits=tw)
                            cpv = r.op("scalar", lambda e, kt0=kt0, vp=vp, bnk=bnk:
                                       e.copy(
                                out=bass.AP(t_of(VS), kt0 * 260 + 65 * (2 * vp),
                                            [[NKT * 260, P],
                                             [260, 4], [65, 2], [1, 64]]),
                                in_=bass.AP(t_of(pT), bnk * 512,
                                            [[1024, P], [P, 4], [64, 2], [1, 64]])),
                                waits=[("T", t1), ("V", vm)])
                            ptwr[bnk] = t1
                            ptrd[bnk] = ("S", cpv)
                            last["vs_copy"] = cpv
                        U(u_vt)
                    for side in range(2):
                        for cbp in range(2):
                            def u_st(side=side, cbp=cbp, base=base):
                                tens = QT if side == 0 else KT
                                col0 = 0 if side == 0 else 4
                                srcap = tens[:, cbp * T + base: cbp * T + base + 512]
                                sv = r.op("vector", lambda e, s=srcap: e.tensor_tensor(
                                    out=sqb[:, :], in0=s, in1=s, op=ALU.mult),
                                    waits=[("V", last["qkev_" + str(side * 2 + cbp)]),
                                           ("T", last["sq_t"])])
                                bnks = [pt_bank(), pt_bank()]
                                tmn = 0
                                for hh in range(2):
                                    tw = [("V", sv), ("V", c_oc), ptrd[bnks[hh]]]
                                    tmn = r.op("tensor", lambda e, hh=hh,
                                               b=bnks[hh]: e.matmul(
                                        pT[0:1, b * 512: b * 512 + 512],
                                        onesc_b[64 * hh:64 * hh + 64, 0:1],
                                        sqb[64 * hh:64 * hh + 64, :],
                                        start=True, stop=True,
                                        tile_position=(64 * hh, 0)), waits=tw)
                                    ptwr[bnks[hh]] = tmn
                                last["sq_t"] = tmn
                                rv = 0
                                for hh in range(2):
                                    h = 2 * cbp + hh
                                    rv = r.op("vector", lambda e, h=h,
                                              b=bnks[hh]: e.tensor_reduce(
                                        out=lnst[0:1, 8 + h:9 + h],
                                        in_=pT[0:1, b * 512: b * 512 + 512],
                                        axis=AX.X, op=ALU.max),
                                        waits=[("T", tmn), ("V", last["stat_v"])])
                                    mxs = r.op("vector", lambda e, h=h, c=col0:
                                               e.tensor_tensor(
                                        out=statr[0:1, c + h:c + h + 1],
                                        in0=statr[0:1, c + h:c + h + 1],
                                        in1=lnst[0:1, 8 + h:9 + h], op=ALU.max),
                                        waits=[("V", rv), ("V", z0)])
                                    last["stat_v"] = mxs
                                ptrd[bnks[0]] = ("V", last["stat_v"])
                                ptrd[bnks[1]] = ("V", last["stat_v"])
                            U(u_st)

                def u_mhat(G=G):
                    par = G % 4
                    m0 = r.op("vector", lambda e: e.tensor_tensor(
                        out=statr[0:1, 8:12], in0=statr[0:1, 0:4],
                        in1=statr[0:1, 4:8], op=ALU.add),
                        waits=[("V", last["stat_v"])])
                    m1 = r.op("vector", lambda e: e.tensor_reduce(
                        out=statr[0:1, 12:14].rearrange("p (a b) -> p a b", b=1),
                        in_=statr[0:1, 8:12].rearrange("p (a b) -> p a b", b=2),
                        axis=AX.X, op=ALU.max), waits=[("V", m0)])
                    m2 = r.op("vector", lambda e: e.tensor_scalar(
                        out=statr[0:1, 14:16], in0=statr[0:1, 12:14],
                        scalar1=-0.5 * SCALE * 1.01, scalar2=None, op0=ALU.mult),
                        waits=[("V", m1)])
                    m3 = r.op("vector", lambda e: e.tensor_copy(
                        out=mrow[0:1, 0:2], in_=statr[0:1, 14:16]), waits=[("V", m2)])
                    last["stat_v"] = m3
                    bnk = pt_bank()
                    bm = r.op("tensor", lambda e, bnk=bnk: e.matmul(
                        pT[:, bnk * 512: bnk * 512 + 2], onesr_b[0:1, :],
                        mrow[0:1, 0:2], start=True, stop=True),
                        waits=[("V", m3), ptrd[bnk], ("V", c_or),
                               ("S", last["biasM_sp"][par])])
                    bmv = r.op("vector", lambda e, par=par, bnk=bnk: e.tensor_copy(
                        out=biasM[:, par * 2: par * 2 + 2],
                        in_=pT[:, bnk * 512: bnk * 512 + 2]), waits=[("T", bm)])
                    ptwr[bnk] = bm
                    ptrd[bnk] = ("V", bmv)
                    last["biasM_vp"][par] = bmv
                    prod_done[G] = dict(vs=last["vs_copy"], qkev3=last["qkev3"],
                                        bmv=bmv)
                U(u_mhat)

                return units

            def make_dq_units(G):
                units = []
                nw1 = 4 * NDP
                tot = nw1 + NFP

                def u_load(ci):
                    if ci < nw1:
                        g, half = ci // 4, ci % 4
                        gq = r.op("gpsimd", lambda e, g=g, half=half: e.dma_start(
                            out=q1f[:, :], in_=ap_q1[g * P:(g + 1) * P,
                                                     half * 1024:(half + 1) * 1024]),
                            waits=[("V", last["dq_v"])], sem="Dq")
                        sbp = bass.AP(ap_s1.tensor, g * F1 + half * 1024,
                                      [[0, P], [1, 1024]])
                        gs = r.op("gpsimd", lambda e, sbp=sbp: e.dma_start(
                            out=s1bf[:, :], in_=sbp),
                            waits=[("V", last["dq_v"])], sem="Dsc")
                    else:
                        g2 = ci - nw1
                        gq = r.op("gpsimd", lambda e, g2=g2: e.dma_start(
                            out=q1f[:, 0:DM], in_=ap_q2[g2 * P:(g2 + 1) * P, :]),
                            waits=[("V", last["dq_v"])], sem="Dq")
                        sbp = bass.AP(ap_s2.tensor, g2 * DM, [[0, P], [1, DM]])
                        gs = r.op("gpsimd", lambda e, sbp=sbp: e.dma_start(
                            out=s1bf[:, 0:DM], in_=sbp),
                            waits=[("V", last["dq_v"])], sem="Dsc")
                    last["dq_ld"] = (gq, gs)

                def u_comp(ci):
                    gq, gs = last["dq_ld"]
                    if ci < nw1:
                        g, half = ci // 4, ci % 4
                        dq = r.op("vector", lambda e: e.scalar_tensor_tensor(
                            out=w1bf[:, :], in0=q1f[:, :], scalar=-128.0,
                            in1=s1bf[:, :], op0=ALU.add, op1=ALU.mult),
                            waits=[("Dq", gq), ("Dsc", gs), ("Dqo", last["dq_go"])])
                        bnk = pt_bank()
                        tbw = 0
                        for fl in range(8):
                            tw = [("V", dq), ("V", c_bet)]
                            if fl == 0:
                                tw.append(ptrd[bnk])
                            tbw = r.op("tensor", lambda e, g=g, fl=fl, bnk=bnk:
                                       e.matmul(
                                pT[:, bnk * 512 + fl: bnk * 512 + fl + 1],
                                w1bf[:, fl * P:(fl + 1) * P],
                                betb[:, g:g + 1],
                                start=True, stop=True), waits=tw)
                        ab = r.op("vector", lambda e, half=half, bnk=bnk:
                                  e.tensor_tensor(
                            out=bias1[:, half * 8:(half + 1) * 8],
                            in0=bias1[:, half * 8:(half + 1) * 8],
                            in1=pT[:, bnk * 512: bnk * 512 + 8], op=ALU.add),
                            waits=[("T", tbw), ("V", last["b1_v"])])
                        ptwr[bnk] = tbw
                        ptrd[bnk] = ("V", ab)
                        last["b1_v"] = ab
                        gm = r.op("vector", lambda e, g=g: e.tensor_scalar(
                            out=w1bf[:, :], in0=w1bf[:, :],
                            scalar1=gam_sb[:, g:g + 1],
                            scalar2=None, op0=ALU.mult),
                            waits=[("T", tbw), ("V", ab), ("Dpro", pro)])
                        go = r.op("gpsimd", lambda e, g=g, half=half: e.dma_start(
                            out=w1s.ap()[g * P:(g + 1) * P,
                                         half * 1024:(half + 1) * 1024],
                            in_=w1bf[:, :]), waits=[("V", gm)], sem="Dqo")
                        last["dq_v"] = gm
                        last["dq_go"] = go
                    else:
                        g2 = ci - nw1
                        dq = r.op("vector", lambda e: e.scalar_tensor_tensor(
                            out=w1bf[:, 0:DM], in0=q1f[:, 0:DM], scalar=-128.0,
                            in1=s1bf[:, 0:DM], op0=ALU.add, op1=ALU.mult),
                            waits=[("Dq", gq), ("Dsc", gs), ("Dqo", last["dq_go"])])
                        go = r.op("gpsimd", lambda e, g2=g2: e.dma_start(
                            out=w2s.ap()[g2 * P:(g2 + 1) * P, :],
                            in_=w1bf[:, 0:DM]),
                            waits=[("V", dq)], sem="Dqo")
                        last["dq_v"] = dq
                        last["dq_go"] = go

                dqfn[0] = u_comp
                NGH = max(1, NG // 2)
                lo = tot * G // NGH if G < NGH else tot
                hi = tot * (G + 1) // NGH if G < NGH else tot
                for ci in range(lo, hi):
                    def u_a(ci=ci):
                        if last["dq_pend"] is not None:
                            u_comp(last["dq_pend"])
                        u_load(ci)
                        last["dq_pend"] = ci
                    units.append(u_a)
                return units
            pending = []
            pending_dq = []
            dqfn = [None]
            pend_np = [None]
            rs_pend = []
            fed = [0]

            def flush_np():
                if pend_np[0] is not None:
                    f = pend_np[0]
                    pend_np[0] = None
                    f()
                while rs_pend:
                    lo, hi = rs_pend.pop(0)
                    while pending_dq:
                        pending_dq.pop(0)()
                    r.op("gpsimd", lambda e, lo=lo, hi=hi: e.collective_compute(
                        "ReduceScatter", ALU.add,
                        replica_groups=[[0, 1, 2, 3], [4, 5, 6, 7]],
                        ins=[rin.ap()[lo * 4: hi * 4, :].opt()],
                        outs=[rsout.ap()[lo: hi, :].opt()]),
                        waits=[("Drin", last["prsb_d"])], sem="CC")

            def dq_drain():
                while pending_dq:
                    pending_dq.pop(0)()
                if last["dq_pend"] is not None and dqfn[0] is not None:
                    _ci = last["dq_pend"]
                    last["dq_pend"] = None
                    dqfn[0](_ci)

            def feed(upto):
                while fed[0] <= min(upto, NG - 1):
                    for u in make_prod_units(fed[0]):
                        pending.append((fed[0], u))
                    fed[0] += 1

            def flush(grp):
                while pending and pending[0][0] <= grp:
                    pending.pop(0)[1]()

            feed(0)
            flush(0)
            for G in range(NG):
                feed(G + 2)
                flush(G)
                pending_dq += make_dq_units(G)
                qk_ready = prod_done[G]["vs"]
                qk_ready_v = prod_done[G]["qkev3"]
                for qc in (2 * G, 2 * G + 1):
                    nm = [0, 0]
                    qnm = nm
                    for hp in range(2):
                        h0, h1 = 2 * hp, 2 * hp + 1
                        nkt = 4 * qc + 4
                        pvstart = [True, True]
                        tpv = 0
                        pend = None
                        def emit_pv(e_kt, e_off, e_ptsl, e_exp, e_mw):
                            nonlocal tpv
                            for hh, h in ((0, h0), (1, h1)):
                                tw = [("S", e_exp)]
                                if e_mw:
                                    tw.append(("V", e_mw))
                                if pvstart[hh]:
                                    tw.append(("V", last["pPV_free"][hh]))
                                tpv = r.op("tensor", lambda e, kt=e_kt, h=h, hh=hh,
                                           off=e_off, ptsl=e_ptsl,
                                           st=pvstart[hh], sp=(e_kt == nkt - 1):
                                           e.matmul(
                                    pPV[hh][0:65, off:512],
                                    VS[:, kt * 260 + 65 * h: kt * 260 + 65 * h + 65],
                                    PT[ptsl][:, hh * 512 + off: hh * 512 + 512],
                                    start=st, stop=sp), waits=tw)
                                pvstart[hh] = False
                            last["PT_t"][e_ptsl] = tpv
                        for kt in range(nkt):
                            for _ in range(6):
                                if pending:
                                    pending.pop(0)[1]()
                            if kt == 3 and hp == 0:
                                flush_np()
                            if kt % 5 == 4 and pending_dq:
                                pending_dq.pop(0)()
                            sl = kt % 2
                            base_c = sl * 1024
                            off = max(0, P * (kt - 4 * qc))
                            tmm = 0
                            for hh, h in ((0, h0), (1, h1)):
                                tw = []
                                if hh == 0:
                                    tw = [("S", last["pS_exp_sl"][sl]),
                                          ("V", last["pS_free"][2 * sl]),
                                          ("V", last["pS_free"][2 * sl + 1]),
                                          ("S", qk_ready), ("V", qk_ready_v)]
                                tmm = r.op("tensor", lambda e, h=h, kt=kt, hh=hh,
                                           base_c=base_c, qc=qc: e.matmul(
                                    pS[:, base_c + hh * 512: base_c + hh * 512 + 512],
                                    KT[64 * (h % 2):64 * (h % 2) + 64,
                                       (h // 2) * T + kt * P:(h // 2) * T + (kt + 1) * P],
                                    QT[64 * (h % 2):64 * (h % 2) + 64,
                                       (h // 2) * T + qc * 512:(h // 2) * T + (qc + 1) * 512],
                                    start=True, stop=True,
                                    tile_position=(64 * (h % 2), 0)), waits=tw)
                            ptsl = kt % 2
                            exp_s = r.op("scalar", lambda e, ptsl=ptsl, hp=hp,
                                         base_c=base_c, par=G % 4: e.activation(
                                out=PT[ptsl][:, :], in_=pS[:, base_c:base_c + 1024],
                                func=AF.Exp,
                                bias=biasM[:, par * 2 + hp: par * 2 + hp + 1],
                                scale=SCALE),
                                waits=[("T", tmm), ("V", prod_done[G]["bmv"]),
                                       ("T", last["PT_t"][ptsl])])
                            last["pS_exp_sl"][sl] = exp_s
                            last["pS_exp"] = exp_s
                            last["biasM_sp"][G % 4] = exp_s
                            mw = 0
                            if kt >= 4 * qc:
                                for hh in range(2):
                                    mw = r.op("vector", lambda e, ptsl=ptsl, hh=hh,
                                              off=off: e.tensor_tensor(
                                        out=PT[ptsl][:, hh * 512 + off: hh * 512 + off + P],
                                        in0=PT[ptsl][:, hh * 512 + off: hh * 512 + off + P],
                                        in1=trib[:, :], op=ALU.mult),
                                        waits=[("S", exp_s), ("V", c_tri)])
                            if pend is not None:
                                emit_pv(*pend)
                            pend = (kt, off, ptsl, exp_s, mw)
                        emit_pv(*pend)
                        rc0 = r.op("vector", lambda e: e.tensor_copy(
                            out=rrb[0:1, 0:512], in_=pPV[0][64:65, :]),
                            waits=[("T", tpv)])
                        rc1 = r.op("vector", lambda e: e.tensor_copy(
                            out=rrb[0:1, 512:1024], in_=pPV[1][64:65, :]),
                            waits=[("V", rc0)])
                        nbk = pt_bank()
                        tb0 = r.op("tensor", lambda e, b=nbk: e.matmul(
                            pT[0:64, b * 512:(b + 1) * 512], onesr_b[0:1, 0:64],
                            rrb[0:1, 0:512], start=True, stop=True),
                            waits=[("V", rc1), ptrd[nbk]])
                        tb1 = r.op("tensor", lambda e, b=nbk: e.matmul(
                            pT[64:128, b * 512:(b + 1) * 512], onesr_b[0:1, 0:64],
                            rrb[0:1, 512:1024],
                            start=True, stop=True, tile_position=(0, 64)),
                            waits=[("V", rc1)])
                        cpw = r.op("vector", lambda e, b=nbk: e.reciprocal(
                            out=rbsb[:, :], in_=pT[:, b * 512:(b + 1) * 512]),
                            waits=[("T", tb1)])
                        ptwr[nbk] = tb1
                        ptrd[nbk] = ("V", cpw)
                        for hh in range(2):
                            nm[hh] = r.op("vector", lambda e, hh=hh, hp=hp:
                                          e.tensor_tensor(
                                out=ost[64 * hh:64 * hh + 64, hp * 512:(hp + 1) * 512],
                                in0=pPV[hh][0:64, :],
                                in1=rbsb[64 * hh:64 * hh + 64, :], op=ALU.mult),
                                waits=[("V", cpw), ("T", last["ost_t"])])
                            last["pPV_free"][hh] = nm[hh]

                    def do_proj(qc=qc, nm=qnm):
                        tpr = 0
                        for ttg in range(2):
                            evp = 0
                            for tt2 in range(2):
                                tt = ttg * 2 + tt2
                                for half in range(2):
                                    qq = pt_bank()
                                    psp = pT[:, qq * 512: qq * 512 + 512]
                                    for pp in range(2):
                                        tw = [("V", nm[0]), ("V", nm[1]), ("Dwp", gw)]
                                        if pp == 0:
                                            tw.append(ptrd[qq])
                                        tpr = r.op("tensor", lambda e, tt=tt, half=half,
                                                   pp=pp, ps=psp: e.matmul(
                                            ps, ost[:, pp * 512 + tt * P: pp * 512 + (tt + 1) * P],
                                            wpsb[:, pp * DM + half * 512: pp * DM + (half + 1) * 512],
                                            start=(pp == 0), stop=(pp == 1)), waits=tw)
                                    evp = r.op("vector", lambda e, tt2=tt2, half=half,
                                               ps=psp: e.tensor_copy(
                                        out=prsb[:, tt2 * DM + half * 512: tt2 * DM + (half + 1) * 512],
                                        in_=ps), waits=[("T", tpr), ("Drin", last["prsb_d"])])
                                    ptwr[qq] = tpr
                                    ptrd[qq] = ("V", evp)
                            rrow = qc * 512 + ttg * 256
                            dpr = r.op("sync", lambda e, rrow=rrow: e.dma_start(
                                out=rin.ap()[rrow: rrow + 256,
                                             :].rearrange("(a p) c -> p a c", p=P),
                                in_=prsb[:, :].rearrange("p (a c) -> p a c", a=2)),
                                waits=[("V", evp)], sem="Drin")
                            last["prsb_d"] = dpr
                        last["ost_t"] = tpr
                        if pending_dq:
                            pending_dq.pop(0)()
                    pend_np[0] = do_proj
                if G == NG // 2 - 1:
                    rs_pend.append((0, TQ // 2))
                if G == (3 * NG + 3) // 4 - 1:
                    rs_pend.append((TQ // 2, 3 * TQ // 4))
            flush_np()
            dq_drain()

            bfin = last["b1_v"]

            PH = [(s, r.n[s]) for s in ("T", "V", "S", "Dpro", "Dwp", "Dx", "Dx2",
                                        "Dsc", "Drin", "Dq", "Dqo")]
            pro_d = r.n["Dpro"]
            emit_block()

        with contextlib.ExitStack() as dd:
            de = dd.enter_context
            xsd = [de(nc.sbuf_tensor(f"xsd{i}", [P, 4 * DM], F32)) for i in range(2)]
            xnd = [de(nc.sbuf_tensor(f"xnd{i}", [P, 4 * DM], F32)) for i in range(2)]
            hb = [de(nc.sbuf_tensor(f"hb{i}", [P, 4 * DM], BF16)) for i in range(2)]
            hT = [de(nc.sbuf_tensor(f"hT{i}", [P, NDP * 512], BF16)) for i in range(2)]
            yT = de(nc.sbuf_tensor("yT", [P, NFP * 512], BF16))
            zT = de(nc.sbuf_tensor("zT", [P, NDP * 512], BF16))
            w1sb = [de(nc.sbuf_tensor(f"w1sb{i}", [P, NDP * 512], BF16)) for i in range(2)]
            w2sb = [de(nc.sbuf_tensor(f"w2sb{i}", [P, 2 * DM], BF16)) for i in range(2)]
            rsq = [de(nc.sbuf_tensor(f"rsq{i}", [P, 4 * DM], F32)) for i in range(2)]
            bpbt = de(nc.sbuf_tensor("bpbt", [P, DM], F32))
            etm = de(nc.sbuf_tensor("etm", [P, DM], F32))
            ost2 = de(nc.sbuf_tensor("ost2", [P, DM], F32))
            lnst2 = de(nc.sbuf_tensor("lnst2", [P, 16], F32))
            pD = de(nc.psum_tensor("pD", [P, 3072], F32))
            pT2 = de(nc.psum_tensor("pT2", [P, 2048], BF16))

            def ln2_tile(waits0, xsap, outap):
                w0 = r.op("vector", lambda e: e.bn_stats(
                    out=lnst2[:, 0:6], in_=xsap[:, 0:512]), waits=waits0)
                w1 = r.op("vector", lambda e: e.bn_stats(
                    out=lnst2[:, 6:12], in_=xsap[:, 512:1024]), waits=[("V", w0)])
                w2 = r.op("vector", lambda e: e.bn_aggr(
                    out=lnst2[:, 12:14], in_=lnst2[:, 0:12]), waits=[("V", w1)])
                s0 = r.op("scalar", lambda e: e.activation(
                    out=lnst2[:, 14:15], in_=lnst2[:, 13:14], func=AF.Ln, bias=EPS,
                    scale=1.0), waits=[("V", w2)])
                s1_ = r.op("scalar", lambda e: e.activation(
                    out=lnst2[:, 15:16], in_=lnst2[:, 14:15], func=AF.Exp, bias=0.0,
                    scale=-0.5), waits=[("S", s0)])
                return r.op("vector", lambda e: e.tensor_scalar(
                    out=outap, in0=xsap, scalar1=lnst2[:, 12:13],
                    scalar2=lnst2[:, 15:16], op0=ALU.subtract, op1=ALU.mult),
                    waits=[("S", s1_)])

            drs01 = []
            for s2 in range(2):
                if (s2 + 1) * 512 > TQ // 2:
                    break
                drs01.append(r.op("gpsimd", lambda e, s2=s2: e.dma_start(
                    out=rsq[s2][:, :].rearrange("p (a c) -> p a c", a=4),
                    in_=rsout.ap()[s2 * 512:(s2 + 1) * 512, :].rearrange(
                        "(a p) c -> p a c", p=P)),
                    waits=PH + [("CC", 1)], sem="Drs"))

            r.op("gpsimd", lambda e: e.collective_compute(
                "ReduceScatter", ALU.add,
                replica_groups=[[0, 1, 2, 3], [4, 5, 6, 7]],
                ins=[rin.ap()[3 * T // 4: T, :].opt()],
                outs=[rsout.ap()[3 * TQ // 4: TQ, :].opt()]),
                waits=PH, sem="CC")
            dbp = r.op("sync", lambda e: e.dma_start(
                out=bpbt[0:1, 0:DM], in_=ap_bpb[0:1, :]),
                waits=PH, sem="Dpro")
            c_on = r.op("vector", lambda e: e.memset(ost2[0:1, 0:P], 1.0),
                        waits=PH)
            tb = 0
            for hf in range(2):
                tb = r.op("tensor", lambda e, hf=hf: e.matmul(
                    pD[:, hf * 512:(hf + 1) * 512], ost2[0:1, 0:P],
                    bpbt[0:1, hf * 512:(hf + 1) * 512], start=True, stop=True),
                    waits=PH + [("V", c_on), ("Dpro", dbp)])
            bpb_cp = r.op("vector", lambda e: e.tensor_copy(
                out=bpbt[:, :], in_=pD[:, 0:DM]), waits=[("T", tb)])
            dl = dict(hb_t=[0, 0], hT_rd=[0, 0], yT_t=0, zT_t=0, w1_t=[0, 0],
                      w2_t=[0, 0], etm_v=0, rsq_v=[0, 0],
                      pd1_s=[0, 0], pd2_v=[bpb_cp] * 4, xn_ep=[0, 0],
                      out_d=0, pt2_v=[0] * 16, pt2_seq=0,
                      pt2_bank=[0, 0])

            ST = {}

            def front_a(s):
                sl = s % 2
                dxsem = "Dxa" if sl == 0 else "Dxb"
                dx = r.op("sync", lambda e, s=s: e.dma_start(
                    out=xsd[s % 2][:, :].rearrange("p (a c) -> p a c", a=4),
                    in_=ap_xq[s * 512:(s + 1) * 512, :].rearrange(
                        "(a p) c -> p a c", p=P)),
                    waits=PH + [("V", dl["xn_ep"][sl])], sem=dxsem)
                if s < len(drs01):
                    drs = drs01[s]
                else:
                    end = (s + 1) * 512
                    ccn = 1 if end <= TQ // 2 else (2 if end <= 3 * TQ // 4 else 3)
                    drs = r.op("gpsimd", lambda e, s=s: e.dma_start(
                        out=rsq[sl][:, :].rearrange("p (a c) -> p a c", a=4),
                        in_=rsout.ap()[s * 512:(s + 1) * 512, :].rearrange(
                            "(a p) c -> p a c", p=P)),
                        waits=[("CC", ccn), ("V", dl["rsq_v"][sl])], sem="Drs")
                wv = 0
                for tt in range(4):
                    wv = ln2_tile([(dxsem, dx)], xsd[sl][:, tt * DM:(tt + 1) * DM],
                                  xnd[sl][:, tt * DM:(tt + 1) * DM])
                    wv = ln2_tile([("V", wv), ("T", dl["hb_t"][sl])],
                                  xnd[sl][:, tt * DM:(tt + 1) * DM],
                                  hb[sl][:, tt * DM:(tt + 1) * DM])
                ST[s] = dict(sl=sl, dx=dx, dxsem=dxsem, drs=drs, wv=wv)

            def front_b(s):
                sl = ST[s]["sl"]
                hb_v = ST[s]["wv"]
                cp = 0
                t1 = 0
                for tt in range(4):
                    for dp in range(NDP):
                        i = dl["pt2_seq"]
                        k = (i % 2) * 8 + ((i // 2) % 8)
                        dl["pt2_seq"] = i + 1
                        t1 = r.op("tensor", lambda e, tt=tt, dp=dp, k=k, sl=sl:
                                  e.transpose(
                            pT2[:, k * P:(k + 1) * P],
                            hb[sl][:, tt * DM + dp * P: tt * DM + (dp + 1) * P],
                            idb[:, :]),
                            waits=[("V", hb_v), ("V", dl["pt2_v"][k]),
                                   ("V", dl["pt2_bank"][k // 8]), ("V", c_id)])
                        cp = r.op("vector", lambda e, tt=tt, dp=dp, k=k, sl=sl:
                                  e.tensor_copy(
                            out=hT[sl][:, dp * 512 + tt * P: dp * 512 + (tt + 1) * P],
                            in_=pT2[:, k * P:(k + 1) * P]),
                            waits=[("T", t1), ("T", dl["hT_rd"][sl])])
                        dl["pt2_v"][k] = cp
                        dl["pt2_bank"][k // 8] = cp
                ST[s]["hT_v"] = cp
                dl["hb_t"][sl] = t1

            def do_fc1(s):
                sl = ST[s]["sl"]
                hT_v = ST[s]["hT_v"]
                sy = 0
                tm = 0
                for fc in range(8):
                    wl = fc % 2
                    wsem = "Dw1a" if wl == 0 else "Dw1b"
                    dw = r.op("sync", lambda e, fc=fc, w=wl: e.dma_start(
                        out=w1sb[w][:, :].rearrange("p (a c) -> p a c", a=NDP),
                        in_=w1s.ap()[:, fc * 512:(fc + 1) * 512].rearrange(
                            "(dp p) f -> p dp f", p=P)),
                        waits=PH + [("T", dl["w1_t"][wl])], sem=wsem)
                    for fb in range(4):
                        fba = fc * 4 + fb
                        pq = fba % 2
                        psl = pD[:, 2048 + pq * 512: 2048 + (pq + 1) * 512]
                        for dp in range(NDP):
                            tw = [(wsem, dw), ("V", hT_v)]
                            if dp == 0:
                                tw.append(("S", dl["pd1_s"][pq]))
                            tm = r.op("tensor", lambda e, fb=fb, dp=dp, w=wl,
                                      ps=psl, sl=sl: e.matmul(
                                ps, w1sb[w][:, dp * 512 + fb * P: dp * 512 + (fb + 1) * P],
                                hT[sl][:, dp * 512:(dp + 1) * 512],
                                start=(dp == 0), stop=(dp == NDP - 1)), waits=tw)
                        sy = r.op("scalar", lambda e, fba=fba, ps=psl: e.activation(
                            out=yT[:, fba * 512:(fba + 1) * 512], in_=ps,
                            func=AF.Silu, bias=bias1[:, fba:fba + 1], scale=1.0),
                            waits=[("T", tm), ("T", dl["yT_t"]), ("V", bfin)])
                        dl["pd1_s"][pq] = sy
                    dl["w1_t"][wl] = tm
                dl["hT_rd"][sl] = tm
                ST[s]["yT_s"] = sy

            def do_fc2(s):
                yT_s = ST[s]["yT_s"]
                ev = 0
                tm2 = 0
                for wave in range(2):
                    for fpg in range(NFP // 2):
                        w2l = fpg % 2
                        w2sem = "Dw2a" if w2l == 0 else "Dw2b"
                        dw2 = r.op("sync", lambda e, fpg=fpg, w=w2l: e.dma_start(
                            out=w2sb[w][:, :].rearrange("p (a c) -> p a c", a=2),
                            in_=w2s.ap()[fpg * 2 * P:(fpg + 1) * 2 * P, :].rearrange(
                                "(a p) c -> p a c", p=P)),
                            waits=PH + [("T", dl["w2_t"][w2l])], sem=w2sem)
                        for fpl in range(2):
                            fp = fpg * 2 + fpl
                            for ob in range(4):
                                oba = wave * 4 + ob
                                tw = [(w2sem, dw2), ("S", yT_s)]
                                if fp == 0:
                                    tw.append(("V", dl["pd2_v"][ob]))
                                tm2 = r.op("tensor", lambda e, ob=ob, oba=oba, w=w2l,
                                           fp=fp, fpl=fpl: e.matmul(
                                    pD[:, ob * 512:(ob + 1) * 512],
                                    w2sb[w][:, fpl * DM + oba * P: fpl * DM + (oba + 1) * P],
                                    yT[:, fp * 512:(fp + 1) * 512],
                                    start=(fp == 0), stop=(fp == NFP - 1)), waits=tw)
                        dl["w2_t"][w2l] = tm2
                    for ob in range(4):
                        oba = wave * 4 + ob
                        ev = r.op("vector", lambda e, ob=ob, oba=oba: e.tensor_scalar(
                            out=zT[:, oba * 512:(oba + 1) * 512],
                            in0=pD[:, ob * 512:(ob + 1) * 512],
                            scalar1=b2t_sb[:, oba:oba + 1], scalar2=None,
                            op0=ALU.add),
                            waits=[("T", tm2), ("T", dl["zT_t"]), ("Dpro", pro_d)])
                        dl["pd2_v"][ob] = ev
                dl["yT_t"] = tm2
                ST[s]["zT_v"] = ev

            def do_epi(s):
                sl = ST[s]["sl"]
                wv = ST[s]["wv"]
                drs = ST[s]["drs"]
                zT_v = ST[s]["zT_v"]
                e4 = 0
                e2 = 0
                for tt in range(4):
                    bnk = tt % 2
                    ztr = 0
                    for ob in range(NDP):
                        ztr = r.op("tensor", lambda e, ob=ob, tt=tt, bnk=bnk:
                                   e.transpose(
                            pT2[:, bnk * 1024 + ob * P: bnk * 1024 + (ob + 1) * P],
                            zT[:, ob * 512 + tt * P: ob * 512 + (tt + 1) * P],
                            idb[:, :]),
                            waits=[("V", zT_v), ("V", dl["pt2_bank"][bnk])])
                    dl["zT_t"] = ztr
                    e1 = r.op("vector", lambda e, tt=tt, sl=sl: e.tensor_tensor(
                        out=etm[:, :], in0=xsd[sl][:, tt * DM:(tt + 1) * DM],
                        in1=xnd[sl][:, tt * DM:(tt + 1) * DM], op=ALU.add),
                        waits=[("V", wv)])
                    e2 = r.op("vector", lambda e, tt=tt, sl=sl: e.tensor_tensor(
                        out=etm[:, :], in0=etm[:, :],
                        in1=rsq[sl][:, tt * DM:(tt + 1) * DM], op=ALU.add),
                        waits=[("V", e1), ("Drs", drs)])
                    e3 = r.op("vector", lambda e: e.tensor_tensor(
                        out=etm[:, :], in0=etm[:, :], in1=bpbt[:, :], op=ALU.add),
                        waits=[("V", e2), ("V", bpb_cp)])
                    e4 = r.op("vector", lambda e, tt=tt, bnk=bnk:
                              e.scalar_tensor_tensor(
                        out=ost2[:, :],
                        in0=pT2[:, bnk * 1024:(bnk + 1) * 1024],
                        scalar=0.5, in1=etm[:, :], op0=ALU.mult, op1=ALU.add),
                        waits=[("V", e3), ("T", ztr), ("Dout", dl["out_d"])])
                    dl["etm_v"] = e4
                    dl["pt2_bank"][bnk] = e4
                    for kk in range(8):
                        dl["pt2_v"][bnk * 8 + kk] = e4
                    do = r.op("sync", lambda e, s=s, tt=tt: e.dma_start(
                        out=ap_out[s * 512 + tt * P: s * 512 + (tt + 1) * P, :],
                        in_=ost2[:, :]),
                        waits=[("V", e4)], sem="Dout")
                    dl["out_d"] = do
                dl["xn_ep"][sl] = e4
                dl["rsq_v"][sl] = e2

            front_a(0)
            front_b(0)
            for s in range(NST):
                do_fc1(s)
                if s > 0:
                    do_epi(s - 1)
                if s + 1 < NST:
                    front_a(s + 1)
                do_fc2(s)
                if s + 1 < NST:
                    front_b(s + 1)
            do_epi(NST - 1)

            fin = r.n["Dout"]
            r.op("sync", lambda e: e.nop(), waits=[("Dout", fin)], sem="T")
            emit_block()

    return nc


def kernel(**inputs):
    T = 8192
    TQ = T // 4
    x = np.asarray(inputs["L0_attn_x"])
    args = (x, np.asarray(inputs["L0_attn_w_qkv"]), np.asarray(inputs["L0_attn_b_qkv"]),
            np.asarray(inputs["L0_attn_w_proj"]), np.asarray(inputs["L0_attn_b_proj"]),
            np.asarray(inputs["L0_ffn_ln_gamma"]), np.asarray(inputs["L0_ffn_ln_beta"]),
            np.asarray(inputs["L0_ffn_fc1_q"]), np.asarray(inputs["L0_ffn_fc1_scales"]),
            np.asarray(inputs["L0_ffn_fc1_bias"]), np.asarray(inputs["L0_ffn_fc2_q"]),
            np.asarray(inputs["L0_ffn_fc2_scales"]), np.asarray(inputs["L0_ffn_fc2_bias"]))
    nc = build(T)
    in_maps = [make_core_inputs(c, T, *args) for c in range(8)]
    res = run_bass_kernel_spmd(nc, in_maps, core_ids=list(range(8)))
    out = np.empty((2, T, DM), dtype=np.float32)
    for c in range(8):
        b, hg = c // 4, c % 4
        oq = np.asarray(res.results[c]["out_q"], dtype=np.float32)
        out[b, hg * T // 8:(hg + 1) * T // 8, :] = oq[0:TQ // 2]
        out[b, T // 2 + hg * T // 8: T // 2 + (hg + 1) * T // 8, :] = oq[TQ // 2:]
    return out


def make_core_inputs(c, T, x, wqkv, bqkv, wproj, bproj, gam, bet,
                     q1, s1, b1, q2, s2, b2):
    TQ = T // 4
    b, hg = c // 4, c % 4
    cs = slice(QK * hg, QK * (hg + 1))
    return dict(
        xb=np.ascontiguousarray(x[b]),
        xq=np.ascontiguousarray(np.concatenate(
            [x[b, hg * T // 8:(hg + 1) * T // 8, :],
             x[b, T // 2 + hg * T // 8: T // 2 + (hg + 1) * T // 8, :]], axis=0)),
        wq=np.ascontiguousarray(wqkv[:, 0:1024][:, cs]),
        wk=np.ascontiguousarray(wqkv[:, 1024:2048][:, cs]),
        wv=np.ascontiguousarray(wqkv[:, 2048:3072][:, cs]),
        bq=np.ascontiguousarray(bqkv[0:1024][cs].reshape(2, P).T),
        bk=np.ascontiguousarray(bqkv[1024:2048][cs].reshape(2, P).T),
        bv=np.ascontiguousarray(bqkv[2048:3072][cs].reshape(2, P).T),
        wp=np.ascontiguousarray(wproj[cs, :]),
        bpb=np.ascontiguousarray(bproj.reshape(1, DM)),
        gam=np.ascontiguousarray(gam.reshape(NDP, P).T),
        bet=np.ascontiguousarray(bet.reshape(NDP, P).T),
        q1=np.ascontiguousarray(q1),
        s1=np.ascontiguousarray(s1),
        b1t=np.ascontiguousarray(b1.reshape(NFP, P).T),
        q2=np.ascontiguousarray(q2),
        s2=np.ascontiguousarray(s2),
        b2t=np.ascontiguousarray(b2.reshape(NDP, P).T),
        idf=np.eye(P, dtype=np.float32),
        onesr=np.ones((1, P), dtype=np.float32),
        onesc=np.ones((P, 1), dtype=np.float32),
        trim=np.where(np.arange(P)[:, None] <= np.arange(P)[None, :],
                      1.0, 0.0).astype(np.float32),
        cvals=np.array([[EPS, -0.5, SCALE, 1.0, 0.0, 0.0, 0.0, 0.0]],
                       dtype=np.float32),
    )


# version 18
# speedup vs baseline: 1.0855x; 1.0183x over previous
import contextlib
import numpy as np

import concourse.bass as bass
import concourse.mybir as mybir
from concourse.bass_utils import run_bass_kernel_spmd

F32 = mybir.dt.float32
BF16 = mybir.dt.bfloat16
I32 = mybir.dt.int32
AF = mybir.ActivationFunctionType
ALU = mybir.AluOpType
AX = mybir.AxisListType

P = 128
DM = 1024
NDP = DM // P
HD = 64
NHC = 4
QK = NHC * HD
F1 = 4096
NFP = F1 // P
EPS = 1e-5
SCALE = 0.125

ENGS = ("tensor", "vector", "scalar", "sync", "gpsimd")
SEM_OF = dict(tensor="T", vector="V", scalar="S")
DMA_SEMS = ("Dpro", "Dwp", "Dx", "Dx2", "Dsc", "Drin", "Dq", "Dqo",
            "Dw1a", "Dw1b", "Dw2a", "Dw2b", "Dxa", "Dxb", "Drs", "Dout")
ALL_SEMS = ("T", "V", "S", "Gp", "CC") + DMA_SEMS


class Rec:
    def __init__(self):
        self.q = {e: [] for e in ENGS}
        self.n = {s: 0 for s in ALL_SEMS}

    def op(self, eng, fn, waits=(), sem=None):
        s = sem or SEM_OF[eng]
        a = 16 if s.startswith("D") else 1
        if eng == "gpsimd" and s == "Gp":
            pass
        waits = list(waits)
        if eng == "vector":
            waits.append(("V", self.n["V"]))
        elif eng == "scalar":
            waits.append(("S", self.n["S"]))
        self.n[s] += a
        w = tuple((ws, wv) for ws, wv in waits if wv > 0)
        self.q[eng].append((fn, w, s, a))
        return self.n[s]


def t_of(h):
    return h[:, :].tensor


def build(T):
    NG = T // 1024
    NKT = T // 128
    TQ = T // 4
    NST = TQ // 512
    assert NST >= 1 and T % 1024 == 0

    nc = bass.Bass("TRN2", target_bir_lowering=False, debug=False)
    ap_xb = nc.dram_tensor("xb", [T, DM], F32, kind="ExternalInput").ap()
    ap_xq = nc.dram_tensor("xq", [TQ, DM], F32, kind="ExternalInput").ap()
    ap_wq = nc.dram_tensor("wq", [DM, QK], F32, kind="ExternalInput").ap()
    ap_wk = nc.dram_tensor("wk", [DM, QK], F32, kind="ExternalInput").ap()
    ap_wv = nc.dram_tensor("wv", [DM, QK], F32, kind="ExternalInput").ap()
    ap_bq = nc.dram_tensor("bq", [P, 2], F32, kind="ExternalInput").ap()
    ap_bk = nc.dram_tensor("bk", [P, 2], F32, kind="ExternalInput").ap()
    ap_bv = nc.dram_tensor("bv", [P, 2], F32, kind="ExternalInput").ap()
    ap_wp = nc.dram_tensor("wp", [QK, DM], F32, kind="ExternalInput").ap()
    ap_bpb = nc.dram_tensor("bpb", [1, DM], F32, kind="ExternalInput").ap()
    ap_gam = nc.dram_tensor("gam", [P, NDP], F32, kind="ExternalInput").ap()
    ap_bet = nc.dram_tensor("bet", [P, NDP], F32, kind="ExternalInput").ap()
    ap_q1 = nc.dram_tensor("q1", [DM, F1], I32, kind="ExternalInput").ap()
    ap_s1 = nc.dram_tensor("s1", [NDP, F1], F32, kind="ExternalInput").ap()
    ap_b1t = nc.dram_tensor("b1t", [P, NFP], F32, kind="ExternalInput").ap()
    ap_q2 = nc.dram_tensor("q2", [F1, DM], I32, kind="ExternalInput").ap()
    ap_s2 = nc.dram_tensor("s2", [NFP, DM], F32, kind="ExternalInput").ap()
    ap_b2t = nc.dram_tensor("b2t", [P, NDP], F32, kind="ExternalInput").ap()
    ap_idf = nc.dram_tensor("idf", [P, P], F32, kind="ExternalInput").ap()
    ap_onesr = nc.dram_tensor("onesr", [1, P], F32, kind="ExternalInput").ap()
    ap_onesc = nc.dram_tensor("onesc", [P, 1], F32, kind="ExternalInput").ap()
    ap_trim = nc.dram_tensor("trim", [P, P], F32, kind="ExternalInput").ap()
    ap_cv = nc.dram_tensor("cvals", [1, 8], F32, kind="ExternalInput").ap()
    ap_out = nc.dram_tensor("out_q", [TQ, DM], F32, kind="ExternalOutput").ap()

    rin = nc.dram_tensor("rin", [T, DM], F32)
    rsout = nc.dram_tensor("rsout", [TQ, DM], F32)
    w1s = nc.dram_tensor("w1s", [DM, F1], BF16)
    w2s = nc.dram_tensor("w2s", [F1, DM], BF16)

    r = Rec()

    with contextlib.ExitStack() as perm:
        pe = perm.enter_context
        sems = {s: pe(nc.semaphore("sem" + s)) for s in ALL_SEMS}

        def emit_block():
            blk = nc.Block()
            with blk as block:
                def mk(eng_name):
                    def _replay(e):
                        for fn, waits, s, a in r.q[eng_name]:
                            for ws, wv in waits:
                                e.wait_ge(sems[ws], wv)
                            fn(e).then_inc(sems[s], a)
                    return _replay
                block.tensor(mk("tensor"))
                block.vector(mk("vector"))
                block.scalar(mk("scalar"))
                block.sync(mk("sync"))
                block.gpsimd(mk("gpsimd"))
            for e in ENGS:
                r.q[e] = []

        idf_sb = pe(nc.sbuf_tensor("idf_sb", [P, P], F32))
        idb = pe(nc.sbuf_tensor("idb", [P, P], BF16))
        onesr_b = pe(nc.sbuf_tensor("onesr_b", [1, P], BF16))
        onesc_b = pe(nc.sbuf_tensor("onesc_b", [P, 1], BF16))
        trib = pe(nc.sbuf_tensor("trib", [P, P], BF16))
        stag = pe(nc.sbuf_tensor("stag", [P, 2 * P + 2], F32))
        bq_sb = pe(nc.sbuf_tensor("bq_sb", [P, 2], F32))
        bk_sb = pe(nc.sbuf_tensor("bk_sb", [P, 2], F32))
        bv_sb = pe(nc.sbuf_tensor("bv_sb", [P, 2], F32))
        gam_sb = pe(nc.sbuf_tensor("gam_sb", [P, NDP], F32))
        bet_sb = pe(nc.sbuf_tensor("bet_sb", [P, NDP], F32))
        betb = pe(nc.sbuf_tensor("betb", [P, NDP], BF16))
        b1t_sb = pe(nc.sbuf_tensor("b1t_sb", [P, NFP], F32))
        bias1 = pe(nc.sbuf_tensor("bias1", [P, NFP], F32))
        b2t_sb = pe(nc.sbuf_tensor("b2t_sb", [P, NDP], F32))
        biasM = pe(nc.sbuf_tensor("biasM", [P, 8], F32))
        statr = pe(nc.sbuf_tensor("statr", [1, 16], F32))
        ctile = pe(nc.sbuf_tensor("ctile", [P, 8], F32))
        for i, val in enumerate((EPS, -0.5, SCALE, 1.0, 0.0)):
            nc.const_aps.register(F32, val, ctile[:, i:i + 1])

        with contextlib.ExitStack() as sw:
            se = sw.enter_context
            QT = se(nc.sbuf_tensor("QT", [P, 2 * T], BF16))
            KT = se(nc.sbuf_tensor("KT", [P, 2 * T], BF16))
            VS = se(nc.sbuf_tensor("VS", [P, NKT * 260], BF16))
            ost = se(nc.sbuf_tensor("ost", [P, 1024], BF16))
            xs = [se(nc.sbuf_tensor(f"xs{i}", [P, 4 * DM], F32)) for i in range(2)]
            xnb = [se(nc.sbuf_tensor(f"xnb{i}", [P, 2 * DM], F32)) for i in range(2)]
            xnT = se(nc.sbuf_tensor("xnT", [P, NDP * 512], BF16))
            vts = se(nc.sbuf_tensor("vts", [P, 1024], F32))
            PT = [se(nc.sbuf_tensor(f"PT{i}", [P, 1024], BF16)) for i in range(2)]
            sqb = se(nc.sbuf_tensor("sqb", [P, 512], BF16))
            sqb2 = se(nc.sbuf_tensor("sqb2", [P, 512], BF16))
            wsb = se(nc.sbuf_tensor("wsb", [P, NDP * 768], BF16))
            wpsb = se(nc.sbuf_tensor("wpsb", [P, 2 * DM], BF16))
            prsb = se(nc.sbuf_tensor("prsb", [P, 2 * DM], F32))
            lnst = se(nc.sbuf_tensor("lnst", [P, 80], F32))
            rrb = se(nc.sbuf_tensor("rrb", [1, 1024], BF16))
            rbsb = se(nc.sbuf_tensor("rbsb", [P, 512], F32))
            q1f = se(nc.sbuf_tensor("q1f", [P, 1024], F32))
            s1bf = se(nc.sbuf_tensor("s1bf", [P, 1024], F32))
            w1bf = se(nc.sbuf_tensor("w1bf", [P, 1024], BF16))
            mrow = se(nc.sbuf_tensor("mrow", [1, 4], BF16))

            pS = se(nc.psum_tensor("pS", [P, 2048], F32))
            pPV = [se(nc.psum_tensor(f"pPV{i}", [P, 512], F32)) for i in range(2)]
            pT = se(nc.psum_tensor("pT", [P, 1024], F32))

            r.op("sync", lambda e: e.dma_start(
                out=ctile[:, :], in_=bass.AP(ap_cv.tensor, 0, [[0, P], [1, 8]])),
                sem="Dpro")
            for src, dst in ((ap_bq, bq_sb), (ap_bk, bk_sb), (ap_bv, bv_sb),
                             (ap_gam, gam_sb), (ap_bet, bet_sb),
                             (ap_b1t, b1t_sb), (ap_b2t, b2t_sb)):
                r.op("sync", lambda e, s=src, d=dst: e.dma_start(out=d[:, :], in_=s),
                     sem="Dpro")
            r.op("sync", lambda e: e.dma_start(out=idf_sb[:, :], in_=ap_idf), sem="Dpro")
            r.op("sync", lambda e: e.dma_start(out=stag[:, 0:P], in_=ap_trim), sem="Dpro")
            r.op("sync", lambda e: e.dma_start(out=stag[0:1, P:2 * P], in_=ap_onesr),
                 sem="Dpro")
            pro = r.op("sync", lambda e: e.dma_start(
                out=stag[:, 2 * P:2 * P + 1], in_=ap_onesc), sem="Dpro")
            r.op("scalar", lambda e: e.nop(), waits=[("Dpro", pro)])
            c_id = r.op("vector", lambda e: e.tensor_copy(out=idb[:, :], in_=idf_sb[:, :]),
                        waits=[("Dpro", pro)])
            c_or = r.op("vector", lambda e: e.tensor_copy(out=onesr_b[:, :],
                                                          in_=stag[0:1, P:2 * P]))
            c_oc = r.op("vector", lambda e: e.tensor_copy(out=onesc_b[:, :],
                                                          in_=stag[:, 2 * P:2 * P + 1]))
            c_tri = r.op("vector", lambda e: e.tensor_copy(out=trib[:, :],
                                                           in_=stag[:, 0:P]))
            c_bet = r.op("vector", lambda e: e.tensor_copy(out=betb[:, :],
                                                           in_=bet_sb[:, :]))
            for src, cb0 in ((ap_wq, 0), (ap_wk, 2), (ap_wv, 4)):
                dst = wsb[:, :].rearrange("p (dp c) -> p dp c", dp=NDP)[
                    :, :, cb0 * P: cb0 * P + 256]
                r.op("gpsimd", lambda e, s=src, d=dst: e.dma_start(
                    out=d, in_=s.rearrange("(dp p) c -> p dp c", p=P)), sem="Dwp")
            gw = r.op("gpsimd", lambda e: e.dma_start(
                out=wpsb[:, :].rearrange("p (a c) -> p a c", a=2),
                in_=ap_wp.rearrange("(a p) c -> p a c", p=P)), sem="Dwp")
            vm = 0
            for h in range(NHC):
                vm = r.op("vector", lambda e, h=h: e.memset(
                    VS[:, :].rearrange("p (kt s) -> p kt s", kt=NKT)[
                        :, :, 65 * h + 64: 65 * h + 65], 1.0))
            z0 = r.op("vector", lambda e: e.memset(statr[0:1, :], 0.0))
            z1 = r.op("vector", lambda e: e.tensor_copy(
                out=bias1[:, :], in_=b1t_sb[:, :]), waits=[("Dpro", 16 * 12)])

            last = dict(
                xs_v=[0, 0], xnT_t=0, xnb_rd=[0, 0], vs_copy=0, pt_v=[0] * 16, pt_seq=0,
                pt_prev=[0, 0], pS_exp_sl=[0, 0],
                pS_free=[0] * 4, pS_exp=0, pW_free=0, pPV_free=[0, 0],
                PT_t=[0, 0], sq_t0=0, sq_t1=0, ost_t=0, prsb_d=0,
                dq_v=0, dq_go=0, dq_pend=None, dq_ld=(0, 0), b1_v=z1, stat_v=z0,
                biasM_v=0, biasM_s=0,
            )

            def ln_stats(waits0, xsap, c0):
                w0 = r.op("vector", lambda e, c0=c0: e.bn_stats(
                    out=lnst[:, c0:c0 + 6], in_=xsap[:, 0:512]), waits=waits0)
                w1 = r.op("vector", lambda e, c0=c0: e.bn_stats(
                    out=lnst[:, c0 + 6:c0 + 12], in_=xsap[:, 512:1024]),
                    waits=[("V", w0)])
                return r.op("vector", lambda e, c0=c0: e.bn_aggr(
                    out=lnst[:, c0 + 12:c0 + 14], in_=lnst[:, c0:c0 + 12]),
                    waits=[("V", w1)])

            def ln_fin(wstat, xsap, xnap, c0):
                s0 = r.op("scalar", lambda e, c0=c0: e.activation(
                    out=lnst[:, c0 + 14:c0 + 15], in_=lnst[:, c0 + 13:c0 + 14],
                    func=AF.Ln, bias=EPS, scale=1.0), waits=[("V", wstat)])
                s1_ = r.op("scalar", lambda e, c0=c0: e.activation(
                    out=lnst[:, c0 + 15:c0 + 16], in_=lnst[:, c0 + 14:c0 + 15],
                    func=AF.Exp, bias=0.0, scale=-0.5), waits=[("S", s0)])
                return r.op("vector", lambda e, c0=c0: e.tensor_scalar(
                    out=xnap, in0=xsap, scalar1=lnst[:, c0 + 12:c0 + 13],
                    scalar2=lnst[:, c0 + 15:c0 + 16], op0=ALU.subtract,
                    op1=ALU.mult), waits=[("S", s1_)])

            def ln_tile(waits0, xsap, xnap):
                w2 = ln_stats(waits0, xsap, 64)
                return ln_fin(w2, xsap, xnap, 64)

            ptwr = [0, 0]
            ptrd = [("V", 0), ("V", 0)]
            last["lnv"] = [0, 0, 0, 0]
            last["lnw"] = [0, 0, 0, 0]
            last["dx"] = ("Dx", 0)
            last["xnT_s"] = 0
            last["qkev3"] = 0
            last["biasM_vp"] = [0, 0, 0, 0]
            last["biasM_sp"] = [0, 0, 0, 0]
            prod_done = {}

            def pt_bank():
                b = last["pt_seq"] % 2
                last["pt_seq"] += 1
                return b

            def make_prod_units(G):
                units = []

                def U(f):
                    units.append(f)
                for sg in range(2):
                    base = 1024 * G + 512 * sg
                    slot = (2 * G + sg) % 2

                    def u_dma(base=base, slot=slot):
                        dxsem = "Dx" if slot == 0 else "Dx2"
                        dx = r.op("sync", lambda e, b=base, s=slot: e.dma_start(
                            out=xs[s][:, :].rearrange("p (a c) -> p a c", a=4),
                            in_=ap_xb[b:b + 512, :].rearrange(
                                "(a p) c -> p a c", p=P)),
                            waits=[("V", last["xs_v"][slot])], sem=dxsem)
                        last["dx"] = (dxsem, dx)
                    U(u_dma)
                    for tt in range(4):
                        def u_lns(tt=tt, slot=slot):
                            last["lnw"][tt] = ln_stats(
                                [last["dx"]],
                                xs[slot][:, tt * DM:(tt + 1) * DM], 16 * tt)
                        U(u_lns)
                    for tt in range(4):
                        def u_lnf(tt=tt, slot=slot):
                            hs = tt // 2
                            if tt % 2 == 0:
                                r.op("vector", lambda e: e.engine_nop(),
                                     waits=[("T", last["xnb_rd"][hs])])
                            last["lnv"][tt] = ln_fin(
                                last["lnw"][tt],
                                xs[slot][:, tt * DM:(tt + 1) * DM],
                                xnb[hs][:, (tt % 2) * DM:(tt % 2 + 1) * DM], 16 * tt)
                            if tt == 3:
                                last["xs_v"][slot] = last["lnv"][3]
                        U(u_lnf)
                    for tt in range(4):
                        for dpg in range(2):
                            def u_tr(tt=tt, dpg=dpg):
                                bnk = pt_bank()
                                t1 = 0
                                for dpl in range(4):
                                    dp = dpg * 4 + dpl
                                    tw = [("V", last["lnv"][tt]), ("Dpro", pro)]
                                    if dpl == 0:
                                        tw.append(ptrd[bnk])
                                    t1 = r.op("tensor", lambda e, tt=tt, dp=dp,
                                              bnk=bnk, dpl=dpl: e.transpose(
                                        pT[:, bnk * 512 + dpl * P: bnk * 512 + (dpl + 1) * P],
                                        xnb[tt // 2][:, (tt % 2) * DM + dp * P:
                                                      (tt % 2) * DM + (dp + 1) * P],
                                        idf_sb[:, :]), waits=tw)
                                cp = r.op("scalar", lambda e, tt=tt, dpg=dpg,
                                          bnk=bnk: e.copy(
                                    out=bass.AP(t_of(xnT), (dpg * 4) * 512 + tt * P,
                                                [[512 * NDP, P], [512, 4], [1, P]]),
                                    in_=pT[:, bnk * 512: bnk * 512 + 512]),
                                    waits=[("T", t1)])
                                ptwr[bnk] = t1
                                ptrd[bnk] = ("S", cp)
                                last["xnb_rd"][tt // 2] = t1
                                last["xnT_s"] = cp
                            U(u_tr)
                    for cb in range(6):
                        def u_mm(cb=cb, base=base):
                            bnk = pt_bank()
                            ps_reg = pT[:, bnk * 512: bnk * 512 + 512]
                            tm = 0
                            for dp in range(NDP):
                                tw = [("S", last["xnT_s"])]
                                if dp == 0:
                                    tw += [ptrd[bnk], ("Dwp", gw)]
                                tm = r.op("tensor", lambda e, cb=cb, dp=dp,
                                          ps=ps_reg: e.matmul(
                                    ps, wsb[:, dp * 768 + cb * P: dp * 768 + (cb + 1) * P],
                                    xnT[:, dp * 512:(dp + 1) * 512],
                                    start=(dp == 0), stop=(dp == NDP - 1)), waits=tw)
                            if cb < 2:
                                dst = QT[:, cb * T + base: cb * T + base + 512]
                                bap = bq_sb[:, cb:cb + 1]
                            elif cb < 4:
                                dst = KT[:, (cb - 2) * T + base: (cb - 2) * T + base + 512]
                                bap = bk_sb[:, cb - 2:cb - 1]
                            else:
                                dst = vts[:, (cb - 4) * 512:(cb - 4) * 512 + 512]
                                bap = bv_sb[:, cb - 4:cb - 3]
                            ev = r.op("vector", lambda e, d=dst, b=bap, ps=ps_reg:
                                      e.tensor_scalar(out=d, in0=ps, scalar1=b,
                                                      scalar2=None, op0=ALU.add),
                                      waits=[("T", tm), ("Dpro", pro)])
                            ptwr[bnk] = tm
                            ptrd[bnk] = ("V", ev)
                            last["xnT_t"] = tm
                            if cb == 3:
                                last["qkev3"] = ev
                            last["qkev_" + str(cb)] = ev
                        U(u_mm)
                    for vp in range(2):
                        def u_vt(vp=vp, sg=sg):
                            kt0 = G * 8 + sg * 4
                            bnk = pt_bank()
                            t1 = 0
                            for j in range(4):
                                tw = [("V", last["qkev_" + str(4 + vp)]),
                                      ("Dpro", pro)]
                                if j == 0:
                                    tw.append(ptrd[bnk])
                                t1 = r.op("tensor", lambda e, vp=vp, j=j, bnk=bnk:
                                          e.transpose(
                                    pT[:, bnk * 512 + j * P: bnk * 512 + (j + 1) * P],
                                    vts[:, vp * 512 + j * P: vp * 512 + (j + 1) * P],
                                    idf_sb[:, :]), waits=tw)
                            cpv = r.op("scalar", lambda e, kt0=kt0, vp=vp, bnk=bnk:
                                       e.copy(
                                out=bass.AP(t_of(VS), kt0 * 260 + 65 * (2 * vp),
                                            [[NKT * 260, P],
                                             [260, 4], [65, 2], [1, 64]]),
                                in_=bass.AP(t_of(pT), bnk * 512,
                                            [[1024, P], [P, 4], [64, 2], [1, 64]])),
                                waits=[("T", t1), ("V", vm)])
                            ptwr[bnk] = t1
                            ptrd[bnk] = ("S", cpv)
                            last["vs_copy"] = cpv
                        U(u_vt)
                    for side in range(2):
                        for cbp in range(2):
                            def u_st(side=side, cbp=cbp, base=base):
                                tens = QT if side == 0 else KT
                                col0 = 0 if side == 0 else 4
                                buf = sqb if cbp == 0 else sqb2
                                skey = "sq_t" + str(cbp)
                                srcap = tens[:, cbp * T + base: cbp * T + base + 512]
                                sv = r.op("vector", lambda e, s=srcap, buf=buf:
                                          e.tensor_tensor(
                                    out=buf[:, :], in0=s, in1=s, op=ALU.mult),
                                    waits=[("V", last["qkev_" + str(side * 2 + cbp)]),
                                           ("T", last[skey])])
                                bnks = [pt_bank(), pt_bank()]
                                tmn = 0
                                for hh in range(2):
                                    tw = [("V", sv), ("V", c_oc), ptrd[bnks[hh]]]
                                    tmn = r.op("tensor", lambda e, hh=hh, buf=buf,
                                               b=bnks[hh]: e.matmul(
                                        pT[0:1, b * 512: b * 512 + 512],
                                        onesc_b[64 * hh:64 * hh + 64, 0:1],
                                        buf[64 * hh:64 * hh + 64, :],
                                        start=True, stop=True,
                                        tile_position=(64 * hh, 0)), waits=tw)
                                    ptwr[bnks[hh]] = tmn
                                last[skey] = tmn
                                rv = 0
                                for hh in range(2):
                                    h = 2 * cbp + hh
                                    rv = r.op("vector", lambda e, h=h,
                                              b=bnks[hh]: e.tensor_reduce(
                                        out=lnst[0:1, 8 + h:9 + h],
                                        in_=pT[0:1, b * 512: b * 512 + 512],
                                        axis=AX.X, op=ALU.max),
                                        waits=[("T", tmn), ("V", last["stat_v"])])
                                    mxs = r.op("vector", lambda e, h=h, c=col0:
                                               e.tensor_tensor(
                                        out=statr[0:1, c + h:c + h + 1],
                                        in0=statr[0:1, c + h:c + h + 1],
                                        in1=lnst[0:1, 8 + h:9 + h], op=ALU.max),
                                        waits=[("V", rv), ("V", z0)])
                                    last["stat_v"] = mxs
                                ptrd[bnks[0]] = ("V", last["stat_v"])
                                ptrd[bnks[1]] = ("V", last["stat_v"])
                            U(u_st)

                def u_mhat(G=G):
                    par = G % 4
                    m0 = r.op("vector", lambda e: e.tensor_tensor(
                        out=statr[0:1, 8:12], in0=statr[0:1, 0:4],
                        in1=statr[0:1, 4:8], op=ALU.add),
                        waits=[("V", last["stat_v"])])
                    m1 = r.op("vector", lambda e: e.tensor_reduce(
                        out=statr[0:1, 12:14].rearrange("p (a b) -> p a b", b=1),
                        in_=statr[0:1, 8:12].rearrange("p (a b) -> p a b", b=2),
                        axis=AX.X, op=ALU.max), waits=[("V", m0)])
                    m2 = r.op("vector", lambda e: e.tensor_scalar(
                        out=statr[0:1, 14:16], in0=statr[0:1, 12:14],
                        scalar1=-0.5 * SCALE * 1.01, scalar2=None, op0=ALU.mult),
                        waits=[("V", m1)])
                    m3 = r.op("vector", lambda e: e.tensor_copy(
                        out=mrow[0:1, 0:2], in_=statr[0:1, 14:16]), waits=[("V", m2)])
                    last["stat_v"] = m3
                    bnk = pt_bank()
                    bm = r.op("tensor", lambda e, bnk=bnk: e.matmul(
                        pT[:, bnk * 512: bnk * 512 + 2], onesr_b[0:1, :],
                        mrow[0:1, 0:2], start=True, stop=True),
                        waits=[("V", m3), ptrd[bnk], ("V", c_or),
                               ("S", last["biasM_sp"][par])])
                    bmv = r.op("vector", lambda e, par=par, bnk=bnk: e.tensor_copy(
                        out=biasM[:, par * 2: par * 2 + 2],
                        in_=pT[:, bnk * 512: bnk * 512 + 2]), waits=[("T", bm)])
                    ptwr[bnk] = bm
                    ptrd[bnk] = ("V", bmv)
                    last["biasM_vp"][par] = bmv
                    prod_done[G] = dict(vs=last["vs_copy"], qkev3=last["qkev3"],
                                        bmv=bmv)
                U(u_mhat)

                return units

            def make_dq_units(G):
                units = []
                nw1 = 4 * NDP
                tot = nw1 + NFP

                def u_load(ci):
                    if ci < nw1:
                        g, half = ci // 4, ci % 4
                        gq = r.op("gpsimd", lambda e, g=g, half=half: e.dma_start(
                            out=q1f[:, :], in_=ap_q1[g * P:(g + 1) * P,
                                                     half * 1024:(half + 1) * 1024]),
                            waits=[("V", last["dq_v"])], sem="Dq")
                        sbp = bass.AP(ap_s1.tensor, g * F1 + half * 1024,
                                      [[0, P], [1, 1024]])
                        gs = r.op("gpsimd", lambda e, sbp=sbp: e.dma_start(
                            out=s1bf[:, :], in_=sbp),
                            waits=[("V", last["dq_v"])], sem="Dsc")
                    else:
                        g2 = ci - nw1
                        gq = r.op("gpsimd", lambda e, g2=g2: e.dma_start(
                            out=q1f[:, 0:DM], in_=ap_q2[g2 * P:(g2 + 1) * P, :]),
                            waits=[("V", last["dq_v"])], sem="Dq")
                        sbp = bass.AP(ap_s2.tensor, g2 * DM, [[0, P], [1, DM]])
                        gs = r.op("gpsimd", lambda e, sbp=sbp: e.dma_start(
                            out=s1bf[:, 0:DM], in_=sbp),
                            waits=[("V", last["dq_v"])], sem="Dsc")
                    last["dq_ld"] = (gq, gs)

                def u_comp(ci):
                    gq, gs = last["dq_ld"]
                    if ci < nw1:
                        g, half = ci // 4, ci % 4
                        dq = r.op("vector", lambda e: e.scalar_tensor_tensor(
                            out=w1bf[:, :], in0=q1f[:, :], scalar=-128.0,
                            in1=s1bf[:, :], op0=ALU.add, op1=ALU.mult),
                            waits=[("Dq", gq), ("Dsc", gs), ("Dqo", last["dq_go"])])
                        bnk = pt_bank()
                        tbw = 0
                        for fl in range(8):
                            tw = [("V", dq), ("V", c_bet)]
                            if fl == 0:
                                tw.append(ptrd[bnk])
                            tbw = r.op("tensor", lambda e, g=g, fl=fl, bnk=bnk:
                                       e.matmul(
                                pT[:, bnk * 512 + fl: bnk * 512 + fl + 1],
                                w1bf[:, fl * P:(fl + 1) * P],
                                betb[:, g:g + 1],
                                start=True, stop=True), waits=tw)
                        ab = r.op("vector", lambda e, half=half, bnk=bnk:
                                  e.tensor_tensor(
                            out=bias1[:, half * 8:(half + 1) * 8],
                            in0=bias1[:, half * 8:(half + 1) * 8],
                            in1=pT[:, bnk * 512: bnk * 512 + 8], op=ALU.add),
                            waits=[("T", tbw), ("V", last["b1_v"])])
                        ptwr[bnk] = tbw
                        ptrd[bnk] = ("V", ab)
                        last["b1_v"] = ab
                        gm = r.op("vector", lambda e, g=g: e.tensor_scalar(
                            out=w1bf[:, :], in0=w1bf[:, :],
                            scalar1=gam_sb[:, g:g + 1],
                            scalar2=None, op0=ALU.mult),
                            waits=[("T", tbw), ("V", ab), ("Dpro", pro)])
                        go = r.op("gpsimd", lambda e, g=g, half=half: e.dma_start(
                            out=w1s.ap()[g * P:(g + 1) * P,
                                         half * 1024:(half + 1) * 1024],
                            in_=w1bf[:, :]), waits=[("V", gm)], sem="Dqo")
                        last["dq_v"] = gm
                        last["dq_go"] = go
                    else:
                        g2 = ci - nw1
                        dq = r.op("vector", lambda e: e.scalar_tensor_tensor(
                            out=w1bf[:, 0:DM], in0=q1f[:, 0:DM], scalar=-128.0,
                            in1=s1bf[:, 0:DM], op0=ALU.add, op1=ALU.mult),
                            waits=[("Dq", gq), ("Dsc", gs), ("Dqo", last["dq_go"])])
                        go = r.op("gpsimd", lambda e, g2=g2: e.dma_start(
                            out=w2s.ap()[g2 * P:(g2 + 1) * P, :],
                            in_=w1bf[:, 0:DM]),
                            waits=[("V", dq)], sem="Dqo")
                        last["dq_v"] = dq
                        last["dq_go"] = go

                dqfn[0] = u_comp
                NGH = max(1, NG // 2)
                lo = tot * G // NGH if G < NGH else tot
                hi = tot * (G + 1) // NGH if G < NGH else tot
                for ci in range(lo, hi):
                    def u_a(ci=ci):
                        if last["dq_pend"] is not None:
                            u_comp(last["dq_pend"])
                        u_load(ci)
                        last["dq_pend"] = ci
                    units.append(u_a)
                return units
            pending = []
            pending_dq = []
            dqfn = [None]
            pend_np = [None]
            rs_pend = []
            fed = [0]

            def flush_np():
                if pend_np[0] is not None:
                    f = pend_np[0]
                    pend_np[0] = None
                    f()
                while rs_pend:
                    lo, hi = rs_pend.pop(0)
                    while pending_dq:
                        pending_dq.pop(0)()
                    r.op("gpsimd", lambda e, lo=lo, hi=hi: e.collective_compute(
                        "ReduceScatter", ALU.add,
                        replica_groups=[[0, 1, 2, 3], [4, 5, 6, 7]],
                        ins=[rin.ap()[lo * 4: hi * 4, :].opt()],
                        outs=[rsout.ap()[lo: hi, :].opt()]),
                        waits=[("Drin", last["prsb_d"])], sem="CC")

            def dq_drain():
                while pending_dq:
                    pending_dq.pop(0)()
                if last["dq_pend"] is not None and dqfn[0] is not None:
                    _ci = last["dq_pend"]
                    last["dq_pend"] = None
                    dqfn[0](_ci)

            def feed(upto):
                while fed[0] <= min(upto, NG - 1):
                    for u in make_prod_units(fed[0]):
                        pending.append((fed[0], u))
                    fed[0] += 1

            def flush(grp):
                while pending and pending[0][0] <= grp:
                    pending.pop(0)[1]()

            feed(0)
            flush(0)
            for G in range(NG):
                feed(G + 2)
                flush(G)
                pending_dq += make_dq_units(G)
                qk_ready = prod_done[G]["vs"]
                qk_ready_v = prod_done[G]["qkev3"]
                for qc in (2 * G, 2 * G + 1):
                    nm = [0, 0]
                    qnm = nm
                    for hp in range(2):
                        h0, h1 = 2 * hp, 2 * hp + 1
                        nkt = 4 * qc + 4
                        pvstart = [True, True]
                        tpv = 0
                        pend = None
                        def emit_pv(e_kt, e_off, e_ptsl, e_exp, e_mw):
                            nonlocal tpv
                            for hh, h in ((0, h0), (1, h1)):
                                tw = [("S", e_exp)]
                                if e_mw:
                                    tw.append(("V", e_mw))
                                if pvstart[hh]:
                                    tw.append(("V", last["pPV_free"][hh]))
                                tpv = r.op("tensor", lambda e, kt=e_kt, h=h, hh=hh,
                                           off=e_off, ptsl=e_ptsl,
                                           st=pvstart[hh], sp=(e_kt == nkt - 1):
                                           e.matmul(
                                    pPV[hh][0:65, off:512],
                                    VS[:, kt * 260 + 65 * h: kt * 260 + 65 * h + 65],
                                    PT[ptsl][:, hh * 512 + off: hh * 512 + 512],
                                    start=st, stop=sp), waits=tw)
                                pvstart[hh] = False
                            last["PT_t"][e_ptsl] = tpv
                        for kt in range(nkt):
                            if kt >= 2:
                                for _ in range(7):
                                    if pending:
                                        pending.pop(0)[1]()
                            if kt == 3 and hp == 0:
                                flush_np()
                            if kt % 5 == 4 and pending_dq:
                                pending_dq.pop(0)()
                            sl = kt % 2
                            base_c = sl * 1024
                            off = max(0, P * (kt - 4 * qc))
                            tmm = 0
                            for hh, h in ((0, h0), (1, h1)):
                                tw = []
                                if hh == 0:
                                    tw = [("S", last["pS_exp_sl"][sl]),
                                          ("V", last["pS_free"][2 * sl]),
                                          ("V", last["pS_free"][2 * sl + 1]),
                                          ("S", qk_ready), ("V", qk_ready_v)]
                                tmm = r.op("tensor", lambda e, h=h, kt=kt, hh=hh,
                                           base_c=base_c, qc=qc: e.matmul(
                                    pS[:, base_c + hh * 512: base_c + hh * 512 + 512],
                                    KT[64 * (h % 2):64 * (h % 2) + 64,
                                       (h // 2) * T + kt * P:(h // 2) * T + (kt + 1) * P],
                                    QT[64 * (h % 2):64 * (h % 2) + 64,
                                       (h // 2) * T + qc * 512:(h // 2) * T + (qc + 1) * 512],
                                    start=True, stop=True,
                                    tile_position=(64 * (h % 2), 0)), waits=tw)
                            ptsl = kt % 2
                            exp_s = r.op("scalar", lambda e, ptsl=ptsl, hp=hp,
                                         base_c=base_c, par=G % 4: e.activation(
                                out=PT[ptsl][:, :], in_=pS[:, base_c:base_c + 1024],
                                func=AF.Exp,
                                bias=biasM[:, par * 2 + hp: par * 2 + hp + 1],
                                scale=SCALE),
                                waits=[("T", tmm), ("V", prod_done[G]["bmv"]),
                                       ("T", last["PT_t"][ptsl])])
                            last["pS_exp_sl"][sl] = exp_s
                            last["pS_exp"] = exp_s
                            last["biasM_sp"][G % 4] = exp_s
                            mw = 0
                            if kt >= 4 * qc:
                                for hh in range(2):
                                    mw = r.op("vector", lambda e, ptsl=ptsl, hh=hh,
                                              off=off: e.tensor_tensor(
                                        out=PT[ptsl][:, hh * 512 + off: hh * 512 + off + P],
                                        in0=PT[ptsl][:, hh * 512 + off: hh * 512 + off + P],
                                        in1=trib[:, :], op=ALU.mult),
                                        waits=[("S", exp_s), ("V", c_tri)])
                            if pend is not None:
                                emit_pv(*pend)
                            pend = (kt, off, ptsl, exp_s, mw)
                        emit_pv(*pend)
                        rc0 = r.op("vector", lambda e: e.tensor_copy(
                            out=rrb[0:1, 0:512], in_=pPV[0][64:65, :]),
                            waits=[("T", tpv)])
                        rc1 = r.op("vector", lambda e: e.tensor_copy(
                            out=rrb[0:1, 512:1024], in_=pPV[1][64:65, :]),
                            waits=[("V", rc0)])
                        nbk = pt_bank()
                        tb0 = r.op("tensor", lambda e, b=nbk: e.matmul(
                            pT[0:64, b * 512:(b + 1) * 512], onesr_b[0:1, 0:64],
                            rrb[0:1, 0:512], start=True, stop=True),
                            waits=[("V", rc1), ptrd[nbk]])
                        tb1 = r.op("tensor", lambda e, b=nbk: e.matmul(
                            pT[64:128, b * 512:(b + 1) * 512], onesr_b[0:1, 0:64],
                            rrb[0:1, 512:1024],
                            start=True, stop=True, tile_position=(0, 64)),
                            waits=[("V", rc1)])
                        cpw = r.op("vector", lambda e, b=nbk: e.reciprocal(
                            out=rbsb[:, :], in_=pT[:, b * 512:(b + 1) * 512]),
                            waits=[("T", tb1)])
                        ptwr[nbk] = tb1
                        ptrd[nbk] = ("V", cpw)
                        for hh in range(2):
                            nm[hh] = r.op("vector", lambda e, hh=hh, hp=hp:
                                          e.tensor_tensor(
                                out=ost[64 * hh:64 * hh + 64, hp * 512:(hp + 1) * 512],
                                in0=pPV[hh][0:64, :],
                                in1=rbsb[64 * hh:64 * hh + 64, :], op=ALU.mult),
                                waits=[("V", cpw), ("T", last["ost_t"])])
                            last["pPV_free"][hh] = nm[hh]

                    def do_proj(qc=qc, nm=qnm):
                        tpr = 0
                        for ttg in range(2):
                            evp = 0
                            for tt2 in range(2):
                                tt = ttg * 2 + tt2
                                for half in range(2):
                                    qq = pt_bank()
                                    psp = pT[:, qq * 512: qq * 512 + 512]
                                    for pp in range(2):
                                        tw = [("V", nm[0]), ("V", nm[1]), ("Dwp", gw)]
                                        if pp == 0:
                                            tw.append(ptrd[qq])
                                        tpr = r.op("tensor", lambda e, tt=tt, half=half,
                                                   pp=pp, ps=psp: e.matmul(
                                            ps, ost[:, pp * 512 + tt * P: pp * 512 + (tt + 1) * P],
                                            wpsb[:, pp * DM + half * 512: pp * DM + (half + 1) * 512],
                                            start=(pp == 0), stop=(pp == 1)), waits=tw)
                                    evp = r.op("vector", lambda e, tt2=tt2, half=half,
                                               ps=psp: e.tensor_copy(
                                        out=prsb[:, tt2 * DM + half * 512: tt2 * DM + (half + 1) * 512],
                                        in_=ps), waits=[("T", tpr), ("Drin", last["prsb_d"])])
                                    ptwr[qq] = tpr
                                    ptrd[qq] = ("V", evp)
                            rrow = qc * 512 + ttg * 256
                            dpr = r.op("sync", lambda e, rrow=rrow: e.dma_start(
                                out=rin.ap()[rrow: rrow + 256,
                                             :].rearrange("(a p) c -> p a c", p=P),
                                in_=prsb[:, :].rearrange("p (a c) -> p a c", a=2)),
                                waits=[("V", evp)], sem="Drin")
                            last["prsb_d"] = dpr
                        last["ost_t"] = tpr
                        if pending_dq:
                            pending_dq.pop(0)()
                    pend_np[0] = do_proj
                if G == NG // 2 - 1:
                    rs_pend.append((0, TQ // 2))
                if G == (3 * NG + 3) // 4 - 1:
                    rs_pend.append((TQ // 2, 3 * TQ // 4))
            flush_np()
            dq_drain()

            bfin = last["b1_v"]

            PH = [(s, r.n[s]) for s in ("T", "V", "S", "Dpro", "Dwp", "Dx", "Dx2",
                                        "Dsc", "Drin", "Dq", "Dqo")]
            pro_d = r.n["Dpro"]
            emit_block()

        with contextlib.ExitStack() as dd:
            de = dd.enter_context
            xsd = [de(nc.sbuf_tensor(f"xsd{i}", [P, 4 * DM], F32)) for i in range(2)]
            xnd = [de(nc.sbuf_tensor(f"xnd{i}", [P, 4 * DM], F32)) for i in range(2)]
            hb = [de(nc.sbuf_tensor(f"hb{i}", [P, 4 * DM], BF16)) for i in range(2)]
            hT = [de(nc.sbuf_tensor(f"hT{i}", [P, NDP * 512], BF16)) for i in range(2)]
            yT = de(nc.sbuf_tensor("yT", [P, NFP * 512], BF16))
            zT = de(nc.sbuf_tensor("zT", [P, NDP * 512], BF16))
            w1sb = [de(nc.sbuf_tensor(f"w1sb{i}", [P, NDP * 512], BF16)) for i in range(2)]
            w2sb = [de(nc.sbuf_tensor(f"w2sb{i}", [P, 2 * DM], BF16)) for i in range(2)]
            rsq = [de(nc.sbuf_tensor(f"rsq{i}", [P, 4 * DM], F32)) for i in range(2)]
            bpbt = de(nc.sbuf_tensor("bpbt", [P, DM], F32))
            etm = de(nc.sbuf_tensor("etm", [P, DM], F32))
            ost2 = de(nc.sbuf_tensor("ost2", [P, DM], F32))
            lnst2 = de(nc.sbuf_tensor("lnst2", [P, 16], F32))
            pD = de(nc.psum_tensor("pD", [P, 3072], F32))
            pT2 = de(nc.psum_tensor("pT2", [P, 2048], BF16))

            def ln2_tile(waits0, xsap, outap):
                w0 = r.op("vector", lambda e: e.bn_stats(
                    out=lnst2[:, 0:6], in_=xsap[:, 0:512]), waits=waits0)
                w1 = r.op("vector", lambda e: e.bn_stats(
                    out=lnst2[:, 6:12], in_=xsap[:, 512:1024]), waits=[("V", w0)])
                w2 = r.op("vector", lambda e: e.bn_aggr(
                    out=lnst2[:, 12:14], in_=lnst2[:, 0:12]), waits=[("V", w1)])
                s0 = r.op("scalar", lambda e: e.activation(
                    out=lnst2[:, 14:15], in_=lnst2[:, 13:14], func=AF.Ln, bias=EPS,
                    scale=1.0), waits=[("V", w2)])
                s1_ = r.op("scalar", lambda e: e.activation(
                    out=lnst2[:, 15:16], in_=lnst2[:, 14:15], func=AF.Exp, bias=0.0,
                    scale=-0.5), waits=[("S", s0)])
                return r.op("vector", lambda e: e.tensor_scalar(
                    out=outap, in0=xsap, scalar1=lnst2[:, 12:13],
                    scalar2=lnst2[:, 15:16], op0=ALU.subtract, op1=ALU.mult),
                    waits=[("S", s1_)])

            drs01 = []
            for s2 in range(2):
                if (s2 + 1) * 512 > TQ // 2:
                    break
                drs01.append(r.op("gpsimd", lambda e, s2=s2: e.dma_start(
                    out=rsq[s2][:, :].rearrange("p (a c) -> p a c", a=4),
                    in_=rsout.ap()[s2 * 512:(s2 + 1) * 512, :].rearrange(
                        "(a p) c -> p a c", p=P)),
                    waits=PH + [("CC", 1)], sem="Drs"))

            r.op("gpsimd", lambda e: e.collective_compute(
                "ReduceScatter", ALU.add,
                replica_groups=[[0, 1, 2, 3], [4, 5, 6, 7]],
                ins=[rin.ap()[3 * T // 4: T, :].opt()],
                outs=[rsout.ap()[3 * TQ // 4: TQ, :].opt()]),
                waits=PH, sem="CC")
            dbp = r.op("sync", lambda e: e.dma_start(
                out=bpbt[0:1, 0:DM], in_=ap_bpb[0:1, :]),
                waits=PH, sem="Dpro")
            c_on = r.op("vector", lambda e: e.memset(ost2[0:1, 0:P], 1.0),
                        waits=PH)
            tb = 0
            for hf in range(2):
                tb = r.op("tensor", lambda e, hf=hf: e.matmul(
                    pD[:, hf * 512:(hf + 1) * 512], ost2[0:1, 0:P],
                    bpbt[0:1, hf * 512:(hf + 1) * 512], start=True, stop=True),
                    waits=PH + [("V", c_on), ("Dpro", dbp)])
            bpb_cp = r.op("vector", lambda e: e.tensor_copy(
                out=bpbt[:, :], in_=pD[:, 0:DM]), waits=[("T", tb)])
            dl = dict(hb_t=[0, 0], hT_rd=[0, 0], yT_t=0, zT_t=0, w1_t=[0, 0],
                      w2_t=[0, 0], etm_v=0, rsq_v=[0, 0],
                      pd1_s=[0, 0], pd2_v=[bpb_cp] * 4, xn_ep=[0, 0],
                      out_d=0, pt2_v=[0] * 16, pt2_seq=0,
                      pt2_bank=[0, 0])

            ST = {}

            def front_a(s):
                sl = s % 2
                dxsem = "Dxa" if sl == 0 else "Dxb"
                dx = r.op("sync", lambda e, s=s: e.dma_start(
                    out=xsd[s % 2][:, :].rearrange("p (a c) -> p a c", a=4),
                    in_=ap_xq[s * 512:(s + 1) * 512, :].rearrange(
                        "(a p) c -> p a c", p=P)),
                    waits=PH + [("V", dl["xn_ep"][sl])], sem=dxsem)
                if s < len(drs01):
                    drs = drs01[s]
                else:
                    end = (s + 1) * 512
                    ccn = 1 if end <= TQ // 2 else (2 if end <= 3 * TQ // 4 else 3)
                    drs = r.op("gpsimd", lambda e, s=s: e.dma_start(
                        out=rsq[sl][:, :].rearrange("p (a c) -> p a c", a=4),
                        in_=rsout.ap()[s * 512:(s + 1) * 512, :].rearrange(
                            "(a p) c -> p a c", p=P)),
                        waits=[("CC", ccn), ("V", dl["rsq_v"][sl])], sem="Drs")
                wv = 0
                for tt in range(4):
                    wv = ln2_tile([(dxsem, dx)], xsd[sl][:, tt * DM:(tt + 1) * DM],
                                  xnd[sl][:, tt * DM:(tt + 1) * DM])
                    wv = ln2_tile([("V", wv), ("T", dl["hb_t"][sl])],
                                  xnd[sl][:, tt * DM:(tt + 1) * DM],
                                  hb[sl][:, tt * DM:(tt + 1) * DM])
                ST[s] = dict(sl=sl, dx=dx, dxsem=dxsem, drs=drs, wv=wv)

            def front_b(s):
                sl = ST[s]["sl"]
                hb_v = ST[s]["wv"]
                cp = 0
                t1 = 0
                for tt in range(4):
                    for dp in range(NDP):
                        i = dl["pt2_seq"]
                        k = (i % 2) * 8 + ((i // 2) % 8)
                        dl["pt2_seq"] = i + 1
                        t1 = r.op("tensor", lambda e, tt=tt, dp=dp, k=k, sl=sl:
                                  e.transpose(
                            pT2[:, k * P:(k + 1) * P],
                            hb[sl][:, tt * DM + dp * P: tt * DM + (dp + 1) * P],
                            idb[:, :]),
                            waits=[("V", hb_v), ("V", dl["pt2_v"][k]),
                                   ("V", dl["pt2_bank"][k // 8]), ("V", c_id)])
                        cp = r.op("vector", lambda e, tt=tt, dp=dp, k=k, sl=sl:
                                  e.tensor_copy(
                            out=hT[sl][:, dp * 512 + tt * P: dp * 512 + (tt + 1) * P],
                            in_=pT2[:, k * P:(k + 1) * P]),
                            waits=[("T", t1), ("T", dl["hT_rd"][sl])])
                        dl["pt2_v"][k] = cp
                        dl["pt2_bank"][k // 8] = cp
                ST[s]["hT_v"] = cp
                dl["hb_t"][sl] = t1

            def do_fc1(s):
                sl = ST[s]["sl"]
                hT_v = ST[s]["hT_v"]
                sy = 0
                tm = 0
                for fc in range(8):
                    wl = fc % 2
                    wsem = "Dw1a" if wl == 0 else "Dw1b"
                    dw = r.op("sync", lambda e, fc=fc, w=wl: e.dma_start(
                        out=w1sb[w][:, :].rearrange("p (a c) -> p a c", a=NDP),
                        in_=w1s.ap()[:, fc * 512:(fc + 1) * 512].rearrange(
                            "(dp p) f -> p dp f", p=P)),
                        waits=PH + [("T", dl["w1_t"][wl])], sem=wsem)
                    for fb in range(4):
                        fba = fc * 4 + fb
                        pq = fba % 2
                        psl = pD[:, 2048 + pq * 512: 2048 + (pq + 1) * 512]
                        for dp in range(NDP):
                            tw = [(wsem, dw), ("V", hT_v)]
                            if dp == 0:
                                tw.append(("S", dl["pd1_s"][pq]))
                            tm = r.op("tensor", lambda e, fb=fb, dp=dp, w=wl,
                                      ps=psl, sl=sl: e.matmul(
                                ps, w1sb[w][:, dp * 512 + fb * P: dp * 512 + (fb + 1) * P],
                                hT[sl][:, dp * 512:(dp + 1) * 512],
                                start=(dp == 0), stop=(dp == NDP - 1)), waits=tw)
                        sy = r.op("scalar", lambda e, fba=fba, ps=psl: e.activation(
                            out=yT[:, fba * 512:(fba + 1) * 512], in_=ps,
                            func=AF.Silu, bias=bias1[:, fba:fba + 1], scale=1.0),
                            waits=[("T", tm), ("T", dl["yT_t"]), ("V", bfin)])
                        dl["pd1_s"][pq] = sy
                    dl["w1_t"][wl] = tm
                dl["hT_rd"][sl] = tm
                ST[s]["yT_s"] = sy

            def do_fc2(s):
                yT_s = ST[s]["yT_s"]
                ev = 0
                tm2 = 0
                for wave in range(2):
                    for fpg in range(NFP // 2):
                        w2l = fpg % 2
                        w2sem = "Dw2a" if w2l == 0 else "Dw2b"
                        dw2 = r.op("sync", lambda e, fpg=fpg, w=w2l: e.dma_start(
                            out=w2sb[w][:, :].rearrange("p (a c) -> p a c", a=2),
                            in_=w2s.ap()[fpg * 2 * P:(fpg + 1) * 2 * P, :].rearrange(
                                "(a p) c -> p a c", p=P)),
                            waits=PH + [("T", dl["w2_t"][w2l])], sem=w2sem)
                        for fpl in range(2):
                            fp = fpg * 2 + fpl
                            for ob in range(4):
                                oba = wave * 4 + ob
                                tw = [(w2sem, dw2), ("S", yT_s)]
                                if fp == 0:
                                    tw.append(("V", dl["pd2_v"][ob]))
                                tm2 = r.op("tensor", lambda e, ob=ob, oba=oba, w=w2l,
                                           fp=fp, fpl=fpl: e.matmul(
                                    pD[:, ob * 512:(ob + 1) * 512],
                                    w2sb[w][:, fpl * DM + oba * P: fpl * DM + (oba + 1) * P],
                                    yT[:, fp * 512:(fp + 1) * 512],
                                    start=(fp == 0), stop=(fp == NFP - 1)), waits=tw)
                        dl["w2_t"][w2l] = tm2
                    for ob in range(4):
                        oba = wave * 4 + ob
                        ev = r.op("vector", lambda e, ob=ob, oba=oba: e.tensor_scalar(
                            out=zT[:, oba * 512:(oba + 1) * 512],
                            in0=pD[:, ob * 512:(ob + 1) * 512],
                            scalar1=b2t_sb[:, oba:oba + 1], scalar2=None,
                            op0=ALU.add),
                            waits=[("T", tm2), ("T", dl["zT_t"]), ("Dpro", pro_d)])
                        dl["pd2_v"][ob] = ev
                dl["yT_t"] = tm2
                ST[s]["zT_v"] = ev

            def do_epi(s):
                sl = ST[s]["sl"]
                wv = ST[s]["wv"]
                drs = ST[s]["drs"]
                zT_v = ST[s]["zT_v"]
                e4 = 0
                e2 = 0
                for tt in range(4):
                    bnk = tt % 2
                    ztr = 0
                    for ob in range(NDP):
                        ztr = r.op("tensor", lambda e, ob=ob, tt=tt, bnk=bnk:
                                   e.transpose(
                            pT2[:, bnk * 1024 + ob * P: bnk * 1024 + (ob + 1) * P],
                            zT[:, ob * 512 + tt * P: ob * 512 + (tt + 1) * P],
                            idb[:, :]),
                            waits=[("V", zT_v), ("V", dl["pt2_bank"][bnk])])
                    dl["zT_t"] = ztr
                    e1 = r.op("vector", lambda e, tt=tt, sl=sl: e.tensor_tensor(
                        out=etm[:, :], in0=xsd[sl][:, tt * DM:(tt + 1) * DM],
                        in1=xnd[sl][:, tt * DM:(tt + 1) * DM], op=ALU.add),
                        waits=[("V", wv)])
                    e2 = r.op("vector", lambda e, tt=tt, sl=sl: e.tensor_tensor(
                        out=etm[:, :], in0=etm[:, :],
                        in1=rsq[sl][:, tt * DM:(tt + 1) * DM], op=ALU.add),
                        waits=[("V", e1), ("Drs", drs)])
                    e3 = r.op("vector", lambda e: e.tensor_tensor(
                        out=etm[:, :], in0=etm[:, :], in1=bpbt[:, :], op=ALU.add),
                        waits=[("V", e2), ("V", bpb_cp)])
                    e4 = r.op("vector", lambda e, tt=tt, bnk=bnk:
                              e.scalar_tensor_tensor(
                        out=ost2[:, :],
                        in0=pT2[:, bnk * 1024:(bnk + 1) * 1024],
                        scalar=0.5, in1=etm[:, :], op0=ALU.mult, op1=ALU.add),
                        waits=[("V", e3), ("T", ztr), ("Dout", dl["out_d"])])
                    dl["etm_v"] = e4
                    dl["pt2_bank"][bnk] = e4
                    for kk in range(8):
                        dl["pt2_v"][bnk * 8 + kk] = e4
                    do = r.op("sync", lambda e, s=s, tt=tt: e.dma_start(
                        out=ap_out[s * 512 + tt * P: s * 512 + (tt + 1) * P, :],
                        in_=ost2[:, :]),
                        waits=[("V", e4)], sem="Dout")
                    dl["out_d"] = do
                dl["xn_ep"][sl] = e4
                dl["rsq_v"][sl] = e2

            front_a(0)
            front_b(0)
            for s in range(NST):
                do_fc1(s)
                if s > 0:
                    do_epi(s - 1)
                if s + 1 < NST:
                    front_a(s + 1)
                do_fc2(s)
                if s + 1 < NST:
                    front_b(s + 1)
            do_epi(NST - 1)

            fin = r.n["Dout"]
            r.op("sync", lambda e: e.nop(), waits=[("Dout", fin)], sem="T")
            emit_block()

    return nc


def kernel(**inputs):
    T = 8192
    TQ = T // 4
    x = np.asarray(inputs["L0_attn_x"])
    args = (x, np.asarray(inputs["L0_attn_w_qkv"]), np.asarray(inputs["L0_attn_b_qkv"]),
            np.asarray(inputs["L0_attn_w_proj"]), np.asarray(inputs["L0_attn_b_proj"]),
            np.asarray(inputs["L0_ffn_ln_gamma"]), np.asarray(inputs["L0_ffn_ln_beta"]),
            np.asarray(inputs["L0_ffn_fc1_q"]), np.asarray(inputs["L0_ffn_fc1_scales"]),
            np.asarray(inputs["L0_ffn_fc1_bias"]), np.asarray(inputs["L0_ffn_fc2_q"]),
            np.asarray(inputs["L0_ffn_fc2_scales"]), np.asarray(inputs["L0_ffn_fc2_bias"]))
    nc = build(T)
    in_maps = [make_core_inputs(c, T, *args) for c in range(8)]
    res = run_bass_kernel_spmd(nc, in_maps, core_ids=list(range(8)))
    out = np.empty((2, T, DM), dtype=np.float32)
    for c in range(8):
        b, hg = c // 4, c % 4
        oq = np.asarray(res.results[c]["out_q"], dtype=np.float32)
        out[b, hg * T // 8:(hg + 1) * T // 8, :] = oq[0:TQ // 2]
        out[b, T // 2 + hg * T // 8: T // 2 + (hg + 1) * T // 8, :] = oq[TQ // 2:]
    return out


def make_core_inputs(c, T, x, wqkv, bqkv, wproj, bproj, gam, bet,
                     q1, s1, b1, q2, s2, b2):
    TQ = T // 4
    b, hg = c // 4, c % 4
    cs = slice(QK * hg, QK * (hg + 1))
    return dict(
        xb=np.ascontiguousarray(x[b]),
        xq=np.ascontiguousarray(np.concatenate(
            [x[b, hg * T // 8:(hg + 1) * T // 8, :],
             x[b, T // 2 + hg * T // 8: T // 2 + (hg + 1) * T // 8, :]], axis=0)),
        wq=np.ascontiguousarray(wqkv[:, 0:1024][:, cs]),
        wk=np.ascontiguousarray(wqkv[:, 1024:2048][:, cs]),
        wv=np.ascontiguousarray(wqkv[:, 2048:3072][:, cs]),
        bq=np.ascontiguousarray(bqkv[0:1024][cs].reshape(2, P).T),
        bk=np.ascontiguousarray(bqkv[1024:2048][cs].reshape(2, P).T),
        bv=np.ascontiguousarray(bqkv[2048:3072][cs].reshape(2, P).T),
        wp=np.ascontiguousarray(wproj[cs, :]),
        bpb=np.ascontiguousarray(bproj.reshape(1, DM)),
        gam=np.ascontiguousarray(gam.reshape(NDP, P).T),
        bet=np.ascontiguousarray(bet.reshape(NDP, P).T),
        q1=np.ascontiguousarray(q1),
        s1=np.ascontiguousarray(s1),
        b1t=np.ascontiguousarray(b1.reshape(NFP, P).T),
        q2=np.ascontiguousarray(q2),
        s2=np.ascontiguousarray(s2),
        b2t=np.ascontiguousarray(b2.reshape(NDP, P).T),
        idf=np.eye(P, dtype=np.float32),
        onesr=np.ones((1, P), dtype=np.float32),
        onesc=np.ones((P, 1), dtype=np.float32),
        trim=np.where(np.arange(P)[:, None] <= np.arange(P)[None, :],
                      1.0, 0.0).astype(np.float32),
        cvals=np.array([[EPS, -0.5, SCALE, 1.0, 0.0, 0.0, 0.0, 0.0]],
                       dtype=np.float32),
    )
